# version 02 (final)
# baseline (speedup 1.0000x reference)
import contextlib
import numpy as np
import ml_dtypes
import concourse.bass as bass
import concourse.mybir as mybir
from concourse.bass_utils import run_bass_kernel_spmd

F32 = mybir.dt.float32
I32 = mybir.dt.int32
F32R = mybir.dt.float32r
BF16 = mybir.dt.bfloat16
I32 = mybir.dt.int32
AF = mybir.ActivationFunctionType
ALU = mybir.AluOpType
AXX = mybir.AxisListType.X

T = 2048
D = 1024
NTC = 16
NDC = 8
HEADS = 16
EPS_LN = 1e-5

ENGS = ("tensor", "vector", "scalar", "gpsimd", "sync")
NEXP = 8


class Buf:
    __slots__ = ("name", "w", "rs")

    def __init__(self, name):
        self.name = name
        self.w = None
        self.rs = []


class Op:
    __slots__ = ("eng", "fn", "deps", "tok", "inc", "lane")

    def __init__(self, eng, fn, deps, tok, inc, lane=None):
        self.eng, self.fn, self.deps, self.tok, self.inc = eng, fn, deps, tok, inc
        self.lane = lane


NLANES = 4
LANES = [f"g{i}" for i in range(NLANES)]


class Prog:
    def __init__(self):
        self.ops = {e: [] for e in ENGS}
        self.cnt = {e: 0 for e in ENGS}
        for ln in LANES:
            self.cnt[ln] = 0
        self.rr = 0

    def add(self, eng, fn, reads=(), writes=(), extra_deps=(), dma=0):
        deps = {}
        self_wait = eng != "tensor"

        def dep(tok):
            if tok is None:
                return
            de, dn = tok
            if de == eng and not self_wait:
                return
            deps[de] = max(deps.get(de, 0), dn)

        for b in reads:
            dep(b.w)
        for b in writes:
            dep(b.w)
            for r in b.rs:
                dep(r)
        for t in extra_deps:
            dep(t)
        lane = None
        if dma and eng == "gpsimd":
            lane = LANES[self.rr]
            self.rr = (self.rr + 1) % NLANES
            if self.cnt[lane] > 0:
                deps[lane] = max(deps.get(lane, 0), self.cnt[lane])
            self.cnt[lane] += 16 * dma
            tok = (lane, self.cnt[lane])
            self.ops[eng].append(Op(eng, fn, deps, tok, 16 * dma, lane=lane))
        else:
            inc = 16 * dma if dma else 1
            self.cnt[eng] += inc
            tok = (eng, self.cnt[eng])
            self.ops[eng].append(Op(eng, fn, deps, tok, inc))
        for b in reads:
            b.rs.append(tok)
        for b in writes:
            b.w = tok
            b.rs = []
        return tok

    def epoch(self):
        return [(e, self.cnt[e]) for e in list(ENGS) + LANES if self.cnt[e] > 0]

    def emit(self, block, sems):
        def body(e, eng_name):
            for op in self.ops[eng_name]:
                for de, dn in sorted(op.deps.items()):
                    e.wait_ge(sems[de], dn)
                ins = op.fn(e)
                tgt = op.lane if op.lane is not None else eng_name
                if op.inc == 1:
                    ins.then_inc(sems[tgt], 1)
                else:
                    lst = ins if isinstance(ins, (list, tuple)) else [ins]
                    assert len(lst) * 16 == op.inc, (eng_name, len(lst), op.inc)
                    for i in lst:
                        i.then_inc(sems[tgt], 16)
            if eng_name == "sync":
                e.wait_ge(sems["sync"], self.cnt["sync"])
                for ln in LANES:
                    if self.cnt[ln] > 0:
                        e.wait_ge(sems[ln], self.cnt[ln])

        block.tensor(lambda e: body(e, "tensor"))
        block.vector(lambda e: body(e, "vector"))
        block.scalar(lambda e: body(e, "scalar"))
        block.gpsimd(lambda e: body(e, "gpsimd"))
        block.sync(lambda e: body(e, "sync"))


class AView:

    def __init__(self, h, off_bf, cols, dt):
        self.h = h
        self.off = off_bf
        self.cols = cols
        self.dt = dt

    def s(self, a, b):
        return self.h[:, a:b]

    def sr(self, a, b):
        assert self.dt == F32R
        return self.h[:, a:b]

    def all(self):
        return self.s(0, self.cols)


class Arena:
    def __init__(self, nc, t, cols_bf):
        self.nc = nc
        self.base = nc.lookup_mloc(t).addr
        self.free = [(0, cols_bf)]
        self.users = []
        self.live = {}
        self.peak = 0
        self.cols = cols_bf
        self.seq = 0

    def alloc(self, name, cols, dt=BF16):
        need = cols * (2 if dt in (F32, F32R) else 1)
        need = (need + 15) // 16 * 16
        for i, (a, b) in enumerate(self.free):
            if b - a >= need:
                self.free[i] = (a + need, b)
                if self.free[i][0] == self.free[i][1]:
                    self.free.pop(i)
                buf = Buf(name)
                keep = []
                for (ua, ub, ob) in self.users:
                    if ua < a + need and ub > a:
                        if ob.w is not None:
                            buf.rs.append(ob.w)
                        buf.rs.extend(ob.rs)
                        if ua < a or ub > a + need:
                            keep.append((ua, ub, ob))
                    else:
                        keep.append((ua, ub, ob))
                self.users = keep
                self.seq += 1
                h = self.nc.alloc_sbuf_tensor_at(
                    f"ar_{name}_{self.seq}", [128, cols], dt,
                    offset=self.base + 2 * a)
                v = AView(h, a, cols, dt)
                self.live[id(v)] = (a, a + need, buf, name)
                self.peak = max(self.peak, max(e for (s0, e) in [(a, a + need)]))
                return v, buf
        raise RuntimeError(f"arena OOM for {name} ({need} cols); free={self.free}")

    def mark(self, label):
        used = self.cols - sum(b - a for (a, b) in self.free)
        print(f"ARENA {label}: used={used} bf16-cols ({used*2//1024}KB/p), peak={self.peak} ({self.peak*2//1024}KB), free={self.free}")

    def drop(self, view, buf):
        a, b2, _, name = self.live.pop(id(view))
        self.users.append((a, b2, buf))
        self.free.append((a, b2))
        self.free.sort()
        merged = []
        for (x, y) in self.free:
            if merged and merged[-1][1] == x:
                merged[-1] = (merged[-1][0], y)
            else:
                merged.append((x, y))
        self.free = [tuple(m) for m in merged]


class MK:
    def __init__(self, dbg=()):
        self.nc = bass.Bass("TRN2", target_bir_lowering=False)
        self.dbg = set(dbg)
        self.es = contextlib.ExitStack()

    def sb(self, name, p, cols, dt):
        return self.es.enter_context(self.nc.sbuf_tensor(name, [p, cols], dt))

    def din(self, name, shape, dt=F32):
        return self.nc.dram_tensor(name, list(shape), dt, kind="ExternalInput").ap()

    def dout(self, name, shape, dt=F32):
        return self.nc.dram_tensor(name, list(shape), dt, kind="ExternalOutput").ap()


def tap_cols(P, mk, name, ap_fn, rows, cols, b, dt):
    if name not in mk.dbg:
        return
    apo = mk.dout("dbg_" + name, [rows, cols], dt)
    P.add("sync", lambda sy: sy.dma_start(apo, ap_fn()), reads=[b], dma=1)


def tap_tmv(P, mk, name, view, b, ntc=NTC, cols=D):
    if name not in mk.dbg:
        return
    apo = mk.dout("dbg_" + name, [ntc * 128, cols], view.dt)
    src = view.s(0, ntc * cols).rearrange("p (c n) -> p c n", c=ntc)
    P.add("sync", lambda sy: sy.dma_start(
        apo.rearrange("(c p) n -> p c n", p=128), src), reads=[b], dma=1)


def ln_compute_stats(P, src_gen, stat, bStat, dstat, bDstat, scr, bScr, epsap, bC):
    for tc in range(NTC):
        xt, xbufs = src_gen(P, tc)

        def f(v, xt=xt, tc=tc):
            v.scalar_tensor_tensor(scr[:, 0:D], xt, 1.0, xt, ALU.mult, ALU.mult,
                                   accum_out=stat[:, tc:tc + 1])
            return v.reduce_sum(stat[:, 16 + tc:17 + tc], xt, axis=AXX)
        P.add("vector", f, reads=list(xbufs), writes=[bStat, bScr])

    P.add("vector", lambda v: v.tensor_scalar_mul(dstat[:, 16:32], stat[:, 16:32], 1.0 / D),
          reads=[bStat], writes=[bDstat])
    P.add("vector", lambda v: v.tensor_tensor(stat[:, 48:64], dstat[:, 16:32],
                                              dstat[:, 16:32], ALU.mult),
          reads=[bDstat], writes=[bStat])
    P.add("vector", lambda v: v.scalar_tensor_tensor(stat[:, 32:48], stat[:, 0:16], 1.0 / D,
                                                     stat[:, 48:64], ALU.mult, ALU.subtract),
          reads=[bStat], writes=[bStat])
    P.add("scalar", lambda s: s.activation(stat[:, 48:64], stat[:, 32:48], AF.Sqrt, bias=epsap),
          reads=[bStat, bC], writes=[bStat])
    P.add("vector", lambda v: v.reciprocal(dstat[:, 0:16], stat[:, 48:64]),
          reads=[bStat], writes=[bDstat])


def build_core(dbg=(), upto=99):
    mk = MK(dbg)
    nc = mk.nc
    P = Prog()

    xin = mk.din("xin", [T, D])
    oout = mk.dout("oout", [T, D])
    wq_d = mk.din("wq", [D, D]); wk_d = mk.din("wk", [D, D])
    wv_d = mk.din("wv", [D, D]); wo_d = mk.din("wo", [D, D])
    ln1s_d = mk.din("ln1s", [1, D]); ln1b_d = mk.din("ln1b", [1, D])
    ln2s_d = mk.din("ln2s", [1, D]); ln2b_d = mk.din("ln2b", [1, D])
    bo_d = mk.din("bo", [1, D])
    gw_d = mk.din("gw", [D, 8])
    uasp = mk.nc.dram_tensor("uaspill", [128, NTC * D], BF16).ap()
    xnsp = mk.nc.dram_tensor("xnspill", [128, NTC * D], BF16).ap()
    krsp = mk.nc.dram_tensor("kernspill", [128, NTC * D], F32R).ap()
    ldt_d = mk.din("ldt", [128, 512])
    arl_d = mk.din("arl", [128, 512])
    aim_d = mk.din("aim", [128, 512])
    cre_d = mk.din("cre", [128, 512])
    cim_d = mk.din("cim", [128, 512])
    dvec_d = mk.din("dvec", [1, D])
    bout_d = mk.din("bout", [1, 2048])
    wout_d = mk.din("wout", [D, 2048])
    cdftf_d = mk.din("cdftf", [17, 2, 16, 128, 128], F32R)
    cdfti_d = mk.din("cdfti", [16, 17, 2, 128, 128], F32R)
    w1_d = mk.din("w1", [8, D, 2048]); w2_d = mk.din("w2", [8, D, 2048])
    w3_d = mk.din("w3", [8, 2048, D])
    wkl1_d = mk.din("wkl1", [D, D]); wvl1_d = mk.din("wvl1", [D, D])
    wql1_d = mk.din("wql1", [D, D])
    mw0_d = mk.din("mw0", [64, 256]); mw1_d = mk.din("mw1", [256, 64])
    knw_d = mk.din("knw", [1, 64]); qnw_d = mk.din("qnw", [1, 64])
    snw_d = mk.din("snw", [D]); rnw_d = mk.din("rnw", [D])
    atw_d = mk.din("atw", [D, 32]); atb_d = mk.din("atb", [1, 32])
    wgate_d = mk.din("wgate", [3072, D], I32)
    wup_d = mk.din("wup", [3072, D], I32)
    wdown_d = mk.din("wdown", [D, 3072], I32)

    with mk.es:
        XR = mk.sb("XR", 128, NTC * D, F32)
        stat = mk.sb("stat", 128, 64, F32)
        st0 = mk.sb("st0", 128, 32, F32)
        ident = mk.sb("ident", 128, 128, BF16)
        ones_bf = mk.sb("ones_bf", 128, 128, BF16)
        nones = mk.sb("nones", 1, 128, BF16)
        cvals = mk.sb("cvals", 128, 8, F32)
        ONER = mk.sb("oner", 128, 4, F32R)
        OROW = mk.sb("orow", 1, 128, F32R)
        scr = mk.sb("scr", 128, 1024, F32)
        scrx = mk.sb("scrx", 128, 1024, BF16)
        scrb = mk.sb("scrb", 128, 2048, BF16)
        AR = mk.sb("AR", 128, 63488, BF16)

        PB = [mk.es.enter_context(nc.psum_tensor(f"pb{i}", [128, 512], F32)) for i in range(6)]
        PBT = [mk.es.enter_context(nc.psum_tensor(f"pbt{i}", [128, 1024], BF16)) for i in range(2)]
        semh = {e: mk.es.enter_context(nc.semaphore("q_" + e)) for e in ENGS}
        for ln in LANES:
            semh[ln] = mk.es.enter_context(nc.semaphore("q_" + ln))

        A = Arena(nc, AR, 63488)
        bXR = Buf("XR")
        bStat, bSt0, bConst = Buf("stat"), Buf("st0"), Buf("const")
        bScr, bScrx, bScrb = Buf("scr"), Buf("scrx"), Buf("scrb")
        bXnspD, bKrspD, bUaspD = Buf("xnsp_d"), Buf("krsp_d"), Buf("uasp_d")
        bPB = [Buf(f"pb{i}") for i in range(6)]
        bPBT = [Buf(f"pbt{i}") for i in range(2)]

        P.add("sync", lambda sy: sy.dma_start(
            XR[:, :].rearrange("p (c n) -> p c n", c=NTC),
            xin.rearrange("(c p) n -> p c n", p=128)), writes=[bXR], dma=1)

        def consts(g):
            g.memset(cvals[:, 0:1], EPS_LN)
            g.memset(cvals[:, 1:2], 1e-6)
            g.memset(cvals[:, 2:3], 1e-8)
            g.memset(cvals[:, 3:4], 0.0)
            g.memset(cvals[:, 4:5], 1.0)
            g.memset(ident[:, :], 1.0)
            g.affine_select(out=ident[:, :], in_=ident[:, :], compare_op=ALU.is_equal,
                            fill=0.0, base=0, pattern=[[-1, 128]], channel_multiplier=1)
            g.memset(ones_bf[:, :], 1.0)
            return g.memset(nones[:, :], -1.0)
        P.add("gpsimd", consts, writes=[bConst])
        P.add("scalar", lambda sc: sc.activation(
            ONER[:, 0:4], cvals[:, 4:5].broadcast_to((128, 4)), AF.Identity),
            reads=[bConst], writes=[bConst])
        P.add("scalar", lambda sc: sc.activation(OROW[0:1, :], ones_bf[0:1, :], AF.Identity),
              reads=[bConst], writes=[bConst])

        ln_compute_stats(P, lambda P_, tc: (XR[:, tc * D:(tc + 1) * D], [bXR]),
                         stat, bStat, st0, bSt0, scr, bScr, cvals[:, 0:1], bConst)

        if upto >= 3:
            XNS, bXNS = A.alloc("xnspill_sb", NTC * D, BF16)
            for tc in range(NTC):
                P.add("vector", lambda v, tc=tc: v.tensor_scalar(
                    XNS.s(tc * D, (tc + 1) * D), XR[:, tc * D:(tc + 1) * D],
                    st0[:, 16 + tc:17 + tc], st0[:, tc:tc + 1], ALU.subtract, ALU.mult),
                    reads=[bXR, bSt0], writes=[bXNS])
            P.add("sync", lambda sy: sy.dma_start(xnsp, XNS.all()), reads=[bXNS],
                  writes=[bXnspD], dma=1)
            A.drop(XNS, bXNS)

        A.mark("pre-attn"); A.peak = 0
        skipl0 = (40 <= upto < 50)
        if skipl0:
            upto = upto - 40
        if upto >= 1 and not skipl0:
            l0_attn(P, mk, A, XR, bXR, st0, bSt0, stat, bStat, scr, bScr, scrx, bScrx,
                    scrb, bScrb, ident, ones_bf, nones, cvals, bConst, PB, bPB, PBT, bPBT,
                    wq_d, wk_d, wv_d, wo_d, ln1s_d, ln1b_d, bo_d)
        A.mark("post-attn"); A.peak = 0
        if upto >= 2 and not skipl0:
            l0_moe(P, mk, A, XR, bXR, stat, bStat, scr, bScr, scrb, bScrb,
                   ident, ones_bf, cvals, bConst, PB, bPB, PBT, bPBT,
                   ln2s_d, ln2b_d, gw_d, w1_d, w2_d, w3_d)
        A.mark("post-moe"); A.peak = 0
        if upto >= 3 and not skipl0:
            l0_conv(P, mk, A, XR, bXR, stat, bStat, scr, bScr, scrb, bScrb,
                    scrx, bScrx, ident, ones_bf, cvals, bConst, PB, bPB, PBT, bPBT,
                    xnsp, krsp, uasp, bXnspD, bKrspD, bUaspD, ldt_d, arl_d, aim_d, cre_d, cim_d, dvec_d, bout_d,
                    wout_d, cdftf_d, cdfti_d)

        A.mark("post-conv"); A.peak = 0
        if upto >= 4:
            l1_attn(P, mk, A, XR, bXR, stat, bStat, st0, bSt0, scr, bScr, scrx, bScrx,
                    scrb, bScrb, ident, ones_bf, cvals, bConst, ONER, OROW, PB, bPB,
                    PBT, bPBT, xnsp, bXnspD, wkl1_d, wvl1_d, wql1_d, mw0_d, mw1_d,
                    knw_d, qnw_d, snw_d, rnw_d, atw_d, atb_d, upto)
        A.mark("post-l1attn"); A.peak = 0
        if upto >= 7:
            l1_ffn(P, mk, A, XR, bXR, scr, bScr, scrx, bScrx, ident, ones_bf,
                   cvals, bConst, PB, bPB, PBT, bPBT, xnsp, bXnspD,
                   wgate_d, wup_d, wdown_d)
        A.mark("post-l1ffn")
        P.add("sync", lambda sy: sy.dma_start(
            oout.rearrange("(c p) n -> p c n", p=128),
            XR[:, :].rearrange("p (c n) -> p c n", c=NTC)), reads=[bXR], dma=1)

        with nc.Block("main") as blk:
            P.emit(blk, semh)

    return mk


def l0_attn(P, mk, A, XR, bXR, st0, bSt0, stat, bStat, scr, bScr, scrx, bScrx,
            scrb, bScrb, ident, ones_bf, nones, cvals, bConst, PB, bPB, PBT, bPBT,
            wq_d, wk_d, wv_d, wo_d, ln1s_d, ln1b_d, bo_d):
    CC, bCC = A.alloc("attnc", 896 + D + D + 512, BF16)
    m896 = lambda a, b: CC.s(a, b)
    G1 = 896; B1 = 896 + D; DEN = 896 + 2 * D
    bo_r, bBo = A.alloc("bo_r", D, BF16)

    def cinit(g):
        g.memset(CC.s(0, 896), 0.0)
        return g.affine_select(out=CC.s(0, 896), in_=CC.s(0, 896), compare_op=ALU.is_ge,
                               fill=-1e30, base=-384, pattern=[[1, 896]],
                               channel_multiplier=-1)
    P.add("gpsimd", cinit, writes=[bCC])

    P.add("gpsimd", lambda g: [g.dma_start(scrb[0:1, 0:D], ln1s_d),
                               g.dma_start(scrb[0:1, D:2 * D], ln1b_d),
                               g.dma_start(bo_r.s(0, D)[0:1, :], bo_d)],
          writes=[bScrb, bBo], dma=3)

    for i in range(2):
        for hf in range(2):
            k = (2 * i + hf) % 4
            off = [G1, B1][i]
            P.add("tensor", lambda t, i=i, hf=hf, k=k: t.matmul(
                PB[k][:, :], ones_bf[0:1, 0:128],
                scrb[0:1, i * D + hf * 512:i * D + (hf + 1) * 512],
                start=True, stop=True), reads=[bScrb, bConst], writes=[bPB[k]])
            P.add("vector", lambda v, off=off, hf=hf, k=k: v.tensor_copy(
                CC.s(off + hf * 512, off + (hf + 1) * 512), PB[k][:, :]),
                reads=[bPB[k]], writes=[bCC])

    def xn_src(P_, tc):
        P_.add("vector", lambda v, tc=tc: v.tensor_scalar(
            scrx[:, 0:D], XR[:, tc * D:(tc + 1) * D],
            st0[:, 16 + tc:17 + tc], st0[:, tc:tc + 1], ALU.subtract, ALU.mult),
            reads=[bXR, bSt0], writes=[bScrx])
        return scrx[:, 0:D], [bScrx]
    ln_compute_stats(P, xn_src, stat, bStat, stat, bStat, scr, bScr, cvals[:, 0:1], bConst)

    L1T, bL1T = A.alloc("l1h_tm", NTC * D, BF16)
    for tc in range(NTC):
        P.add("vector", lambda v, tc=tc: v.tensor_scalar(
            scrx[:, 0:D], XR[:, tc * D:(tc + 1) * D],
            st0[:, 16 + tc:17 + tc], st0[:, tc:tc + 1], ALU.subtract, ALU.mult),
            reads=[bXR, bSt0], writes=[bScrx])
        P.add("vector", lambda v, tc=tc: v.tensor_scalar(
            scr[:, 0:D], scrx[:, 0:D], stat[:, 16 + tc:17 + tc], stat[:, tc:tc + 1],
            ALU.subtract, ALU.mult), reads=[bScrx, bStat], writes=[bScr])
        P.add("vector", lambda v: v.tensor_tensor(scr[:, 0:D], scr[:, 0:D],
                                                  CC.s(G1, G1 + D), ALU.mult),
              reads=[bCC], writes=[bScr])
        P.add("vector", lambda v, tc=tc: v.tensor_tensor(
            L1T.s(tc * D, (tc + 1) * D), scr[:, 0:D], CC.s(B1, B1 + D), ALU.add),
            reads=[bScr, bCC], writes=[bL1T])
    tap_tmv(P, mk, "ln1h", L1T, bL1T)

    L1F, bL1F = A.alloc("l1h_fm", NDC * T, BF16)
    n = 0
    for tc in range(NTC):
        for c in range(NDC):
            k = n % 2
            hb = 512 * ((n // 2) % 2)
            P.add("tensor", lambda t, tc=tc, c=c, k=k, hb=hb: t.transpose(
                PBT[k][:, hb:hb + 128], L1T.s(tc * D + c * 128, tc * D + (c + 1) * 128),
                ident[:, :]), reads=[bL1T, bConst], writes=[bPBT[k]])
            P.add("vector", lambda v, tc=tc, c=c, k=k, hb=hb: v.tensor_copy(
                L1F.s(c * T + tc * 128, c * T + (tc + 1) * 128), PBT[k][:, hb:hb + 128]),
                reads=[bPBT[k]], writes=[bL1F])
            n += 1
    A.drop(L1T, bL1T)

    CTX, bCTX = A.alloc("ctx", NDC * T, BF16)

    QF, bQF = A.alloc("qf", T, BF16)
    KF, bKF = A.alloc("kf", T, BF16)
    V65, bV65 = A.alloc("v65", NTC * 130, BF16)
    WGa, bWGa = A.alloc("wga", 8 * 128, BF16)
    WGb, bWGb = A.alloc("wgb", 8 * 128, BF16)
    PSC, bPSC = A.alloc("psc", 2 * 512, BF16)
    rowq, bRowq = A.alloc("rowq", T, BF16)
    posM, bPosM = A.alloc("posm", T, BF16)

    P.add("vector", lambda v: v.memset(
        V65.all().rearrange("p (t h c) -> p t h c", t=NTC, h=2)[:, :, :, 64:65], 1.0),
        writes=[bV65])

    for gi in range(8):
        P.add("gpsimd", lambda g, gi=gi: g.dma_start(
            WGa.all().rearrange("p (k j) -> p k j", k=8),
            wq_d[:, gi * 128:(gi + 1) * 128].rearrange("(k p) j -> p k j", p=128)),
            writes=[bWGa], dma=1)
        P.add("gpsimd", lambda g, gi=gi: g.dma_start(
            WGb.all().rearrange("p (k j) -> p k j", k=8),
            wk_d[:, gi * 128:(gi + 1) * 128].rearrange("(k p) j -> p k j", p=128)),
            writes=[bWGb], dma=1)

        for ws, bw, dst, bdst in ((WGa, bWGa, QF, bQF), (WGb, bWGb, KF, bKF)):
            for ts in range(4):
                k = ts % 4

                def fmm(t, ws=ws, ts=ts, k=k):
                    for kc in range(NDC):
                        ins = t.matmul(PB[k][:, :], ws.s(kc * 128, (kc + 1) * 128),
                                       L1F.s(kc * T + ts * 512, kc * T + (ts + 1) * 512),
                                       start=(kc == 0), stop=(kc == NDC - 1))
                    return ins
                P.add("tensor", fmm, reads=[bw, bL1F], writes=[bPB[k]])
                P.add("scalar", lambda s, dst=dst, ts=ts, k=k: s.activation(
                    dst.s(ts * 512, (ts + 1) * 512), PB[k][:, :], AF.Identity),
                    reads=[bPB[k]], writes=[bdst])

        P.add("gpsimd", lambda g, gi=gi: g.dma_start(
            WGa.all().rearrange("p (k j) -> p k j", k=8),
            wv_d[:, gi * 128:(gi + 1) * 128].rearrange("(k p) j -> p k j", p=128)),
            writes=[bWGa], dma=1)
        for tc in range(NTC):
            k = 4 + tc % 2

            def fmm(t, tc=tc, k=k):
                for kc in range(NDC):
                    ins = t.matmul(PB[k][:, 0:128],
                                   L1F.s(kc * T + tc * 128, kc * T + (tc + 1) * 128),
                                   WGa.s(kc * 128, (kc + 1) * 128),
                                   start=(kc == 0), stop=(kc == NDC - 1))
                return ins
            P.add("tensor", fmm, reads=[bWGa, bL1F], writes=[bPB[k]])
            P.add("vector", lambda v, tc=tc, k=k: v.tensor_copy(
                V65.s(tc * 130, tc * 130 + 130).rearrange("p (h c) -> p h c", h=2)[:, :, 0:64],
                PB[k][:, 0:128].rearrange("p (h c) -> p h c", h=2)),
                reads=[bPB[k]], writes=[bV65])

        for hg in range(2):
            rb = 64 * hg
            P.add("scalar", lambda s, rb=rb: s.activation(
                scrb[rb:rb + 64, 0:T], QF.all()[rb:rb + 64, :], AF.Square),
                reads=[bQF], writes=[bScrb])
            for qs in range(4):
                P.add("tensor", lambda t, rb=rb, qs=qs: t.matmul(
                    PB[4][0:1, :], ones_bf[rb:rb + 64, 0:1],
                    scrb[rb:rb + 64, qs * 512:(qs + 1) * 512], start=True, stop=True),
                    reads=[bScrb, bConst], writes=[bPB[4]])
                P.add("scalar", lambda s, qs=qs: s.activation(
                    rowq.s(qs * 512, (qs + 1) * 512)[0:1, :], PB[4][0:1, :], AF.Identity),
                    reads=[bPB[4]], writes=[bRowq])
            P.add("scalar", lambda s, rb=rb: s.activation(
                scrb[rb:rb + 64, 0:T], KF.all()[rb:rb + 64, :], AF.Square),
                reads=[bKF, bRowq], writes=[bScrb])
            for qs in range(4):
                P.add("tensor", lambda t, rb=rb, qs=qs: t.matmul(
                    PB[5][0:1, :], ones_bf[rb:rb + 64, 0:1],
                    scrb[rb:rb + 64, qs * 512:(qs + 1) * 512], start=True, stop=True),
                    reads=[bScrb, bConst], writes=[bPB[5]])
                P.add("vector", lambda v, qs=qs: v.reduce_max(
                    stat[0:1, 32 + qs:33 + qs], PB[5][0:1, :], axis=AXX),
                    reads=[bPB[5]], writes=[bStat])
            P.add("vector", lambda v: (v.tensor_tensor(stat[0:1, 36:37], stat[0:1, 32:33],
                                                       stat[0:1, 33:34], ALU.max),
                                       v.tensor_tensor(stat[0:1, 37:38], stat[0:1, 34:35],
                                                       stat[0:1, 35:36], ALU.max))[-1],
                  reads=[bStat], writes=[bStat])
            P.add("vector", lambda v: v.tensor_tensor(stat[0:1, 38:39], stat[0:1, 36:37],
                                                      stat[0:1, 37:38], ALU.max),
                  reads=[bStat], writes=[bStat])
            P.add("vector", lambda v: v.tensor_scalar_mul(stat[0:1, 33:34],
                                                          stat[0:1, 38:39], 0.36),
                  reads=[bStat], writes=[bStat])
            P.add("scalar", lambda s: s.activation(
                posM.s(0, T)[0:1, :], rowq.s(0, T)[0:1, :], AF.Sqrt,
                scale=stat[0:1, 33:34]), reads=[bRowq, bStat], writes=[bPosM])

            for qs in range(4):
                nkt = 4 * qs + 4
                for kt in range(nkt):
                    sb_ = kt % 2

                    def fS(t, rb=rb, kt=kt, qs=qs, sb_=sb_):
                        t.matmul(PB[sb_][:, :], KF.all()[rb:rb + 64, kt * 128:(kt + 1) * 128],
                                 QF.all()[rb:rb + 64, qs * 512:(qs + 1) * 512],
                                 start=True, stop=False)
                        return t.matmul(PB[sb_][:, :], nones[0:1, :],
                                        posM.s(qs * 512, (qs + 1) * 512)[0:1, :],
                                        start=False, stop=True)
                    P.add("tensor", fS, reads=[bKF, bQF, bPosM, bConst], writes=[bPB[sb_]])

                    if kt >= 4 * qs:
                        r = kt - 4 * qs
                        P.add("vector", lambda v, sb_=sb_, r=r: v.tensor_tensor(
                            PB[sb_][:, :], PB[sb_][:, :],
                            CC.s(384 - 128 * r, 896 - 128 * r), ALU.add),
                            reads=[bCC], writes=[bPB[sb_]])

                    P.add("scalar", lambda s, sb_=sb_, kt=kt: s.activation(
                        PSC.s((kt % 2) * 512, (kt % 2 + 1) * 512), PB[sb_][:, :], AF.Exp,
                        scale=0.125), reads=[bPB[sb_]], writes=[bPSC])

                    def fPV(t, kt=kt, hg=hg, nkt=nkt):
                        return t.matmul(PB[2][0:65, :],
                                        V65.s(kt * 130 + hg * 65, kt * 130 + (hg + 1) * 65),
                                        PSC.s((kt % 2) * 512, (kt % 2 + 1) * 512),
                                        start=(kt == 0), stop=(kt == nkt - 1))
                    P.add("tensor", fPV, reads=[bV65, bPSC], writes=[bPB[2]])

                P.add("vector", lambda v: v.tensor_single_scalar(
                    scr[64:65, 0:512], PB[2][64:65, :], 1e-35, ALU.add),
                    reads=[bPB[2]], writes=[bScr])
                P.add("vector", lambda v: v.reciprocal(scr[64:65, 512:1024], scr[64:65, 0:512]),
                      reads=[bScr], writes=[bScr])
                P.add("vector", lambda v: v.tensor_copy(CC.s(DEN, DEN + 512)[64:65, :],
                                                        scr[64:65, 512:1024]),
                      reads=[bScr], writes=[bCC])
                P.add("tensor", lambda t: t.matmul(PB[3][0:64, :], ones_bf[64:65, 0:64],
                                                   CC.s(DEN, DEN + 512)[64:65, :],
                                                   start=True, stop=True),
                      reads=[bCC, bConst], writes=[bPB[3]])
                P.add("vector", lambda v: v.tensor_copy(CC.s(DEN, DEN + 512)[0:64, :],
                                                        PB[3][0:64, :]),
                      reads=[bPB[3]], writes=[bCC])
                P.add("vector", lambda v, rb=rb, qs=qs, gi=gi: v.tensor_tensor(
                    CTX.s(gi * T + qs * 512, gi * T + (qs + 1) * 512)[rb:rb + 64, :],
                    PB[2][0:64, :], CC.s(DEN, DEN + 512)[0:64, :], ALU.mult),
                    reads=[bPB[2], bCC], writes=[bCTX])

    for vv, bb in ((QF, bQF), (KF, bKF), (V65, bV65), (WGa, bWGa), (WGb, bWGb),
                   (PSC, bPSC), (rowq, bRowq), (posM, bPosM)):
        A.drop(vv, bb)

    X4, bX4 = A.alloc("x4", NTC * D, BF16)
    WOs, bWO = A.alloc("wos", 8 * D, BF16)
    P.add("gpsimd", lambda g: g.dma_start(
        WOs.all().rearrange("p (k j) -> p k j", k=8),
        wo_d.rearrange("(k p) j -> p k j", p=128)), writes=[bWO], dma=1)
    for tc in range(NTC):
        for hf in range(2):
            k = (tc * 2 + hf) % 4

            def fmm(t, tc=tc, hf=hf, k=k):
                for kc in range(NDC):
                    t.matmul(PB[k][:, :], CTX.s(kc * T + tc * 128, kc * T + (tc + 1) * 128),
                             WOs.s(kc * D + hf * 512, kc * D + (hf + 1) * 512),
                             start=(kc == 0), stop=False)
                return t.matmul(PB[k][:, :], ones_bf[0:1, 0:128],
                                bo_r.s(hf * 512, (hf + 1) * 512)[0:1, :],
                                start=False, stop=True)
            P.add("tensor", fmm, reads=[bCTX, bWO, bBo, bConst], writes=[bPB[k]])

            sl = slice(tc * D + hf * 512, tc * D + (hf + 1) * 512)
            P.add("vector", lambda v, tc=tc, sl=sl: v.tensor_scalar(
                scrx[:, 0:512], XR[:, sl], st0[:, 16 + tc:17 + tc], st0[:, tc:tc + 1],
                ALU.subtract, ALU.mult), reads=[bXR, bSt0], writes=[bScrx])
            P.add("vector", lambda v, k=k, sl=sl: v.tensor_tensor(
                X4.s(sl.start, sl.stop), PB[k][:, :], scrx[:, 0:512], ALU.add),
                reads=[bPB[k], bScrx], writes=[bX4])
            P.add("vector", lambda v, sl=sl: v.tensor_tensor(
                XR[:, sl], XR[:, sl], X4.s(sl.start, sl.stop), ALU.add),
                reads=[bX4], writes=[bXR])

    tap_tmv(P, mk, "x4", X4, bX4)
    if "xr1" in mk.dbg:
        apo = mk.dout("dbg_xr1", [T, D], F32)
        P.add("sync", lambda sy, apo=apo: sy.dma_start(
            apo.rearrange("(c p) n -> p c n", p=128),
            XR[:, :].rearrange("p (c n) -> p c n", c=NTC)), reads=[bXR], dma=1)

    for vv, bb in ((CTX, bCTX), (L1F, bL1F), (WOs, bWO), (CC, bCC), (bo_r, bBo)):
        A.drop(vv, bb)
    mk.X4, mk.bX4 = X4, bX4
    mk.A = A


def l0_moe(P, mk, A, XR, bXR, stat, bStat, scr, bScr, scrb, bScrb,
           ident, ones_bf, cvals, bConst, PB, bPB, PBT, bPBT,
           ln2s_d, ln2b_d, gw_d, w1_d, w2_d, w3_d):
    X4, bX4 = mk.X4, mk.bX4
    GB, bGB = A.alloc("g2b2", 2 * D + 64, BF16)
    GWO = 2 * D
    P.add("gpsimd", lambda g: [g.dma_start(scrb[0:1, 0:D], ln2s_d),
                               g.dma_start(scrb[0:1, D:2 * D], ln2b_d),
                               g.dma_start(GB.s(GWO, GWO + 64).rearrange("p (k j) -> p k j", k=8),
                                           gw_d.rearrange("(k p) j -> p k j", p=128))],
          writes=[bScrb, bGB], dma=3)
    for i in range(2):
        for hf in range(2):
            k = (2 * i + hf) % 4
            P.add("tensor", lambda t, i=i, hf=hf, k=k: t.matmul(
                PB[k][:, :], ones_bf[0:1, 0:128],
                scrb[0:1, i * D + hf * 512:i * D + (hf + 1) * 512],
                start=True, stop=True), reads=[bScrb, bConst], writes=[bPB[k]])
            P.add("vector", lambda v, i=i, hf=hf, k=k: v.tensor_copy(
                GB.s(i * D + hf * 512, i * D + (hf + 1) * 512), PB[k][:, :]),
                reads=[bPB[k]], writes=[bGB])

    ln_compute_stats(P, lambda P_, tc: (X4.s(tc * D, (tc + 1) * D), [bX4]),
                     stat, bStat, stat, bStat, scr, bScr, cvals[:, 0:1], bConst)
    for tc in range(NTC):
        P.add("vector", lambda v, tc=tc: v.tensor_scalar(
            scr[:, 0:D], X4.s(tc * D, (tc + 1) * D),
            stat[:, 16 + tc:17 + tc], stat[:, tc:tc + 1], ALU.subtract, ALU.mult),
            reads=[bX4, bStat], writes=[bScr])
        P.add("vector", lambda v: v.tensor_tensor(scr[:, 0:D], scr[:, 0:D],
                                                  GB.s(0, D), ALU.mult),
              reads=[bGB], writes=[bScr])
        P.add("vector", lambda v, tc=tc: v.tensor_tensor(
            X4.s(tc * D, (tc + 1) * D), scr[:, 0:D], GB.s(D, 2 * D), ALU.add),
            reads=[bScr, bGB], writes=[bX4])
    tap_tmv(P, mk, "ln2h", X4, bX4)

    L2F, bL2F = A.alloc("l2h_fm", NDC * T, BF16)
    n = 0
    for tc in range(NTC):
        for c in range(NDC):
            k = n % 2
            hb = 512 * ((n // 2) % 2)
            P.add("tensor", lambda t, tc=tc, c=c, k=k, hb=hb: t.transpose(
                PBT[k][:, hb:hb + 128], X4.s(tc * D + c * 128, tc * D + (c + 1) * 128),
                ident[:, :]), reads=[bX4, bConst], writes=[bPBT[k]])
            P.add("vector", lambda v, tc=tc, c=c, k=k, hb=hb: v.tensor_copy(
                L2F.s(c * T + tc * 128, c * T + (tc + 1) * 128), PBT[k][:, hb:hb + 128]),
                reads=[bPBT[k]], writes=[bL2F])
            n += 1
    A.drop(X4, bX4)

    WTM, bWTM = A.alloc("wtm", NTC * 8, F32)
    for tc in range(NTC):
        def fg(t, tc=tc):
            for kc in range(NDC):
                ins = t.matmul(PB[4][:, 0:8], L2F.s(kc * T + tc * 128, kc * T + (tc + 1) * 128),
                               GB.s(GWO + kc * 8, GWO + (kc + 1) * 8),
                               start=(kc == 0), stop=(kc == NDC - 1))
            return ins
        P.add("tensor", fg, reads=[bL2F, bGB], writes=[bPB[4]])
        P.add("vector", lambda v, tc=tc: v.tensor_copy(scr[:, 0:8], PB[4][:, 0:8]),
              reads=[bPB[4]], writes=[bScr])
        P.add("vector", lambda v: v.reduce_max(scr[:, 8:9], scr[:, 0:8], axis=AXX, negate=True),
              reads=[bScr], writes=[bScr])
        P.add("scalar", lambda s: s.activation(scr[:, 16:24], scr[:, 0:8], AF.Exp,
                                               bias=scr[:, 8:9]),
              reads=[bScr], writes=[bScr])
        P.add("vector", lambda v: v.tensor_scalar_mul(scr[:, 9:10], scr[:, 8:9], -1.0),
              reads=[bScr], writes=[bScr])
        P.add("vector", lambda v: v.tensor_scalar(scr[:, 24:32], scr[:, 0:8], scr[:, 9:10],
                                                  None, ALU.is_equal),
              reads=[bScr], writes=[bScr])
        P.add("vector", lambda v: v.scalar_tensor_tensor(scr[:, 24:32], scr[:, 24:32], -1e30,
                                                         scr[:, 0:8], ALU.mult, ALU.add),
              reads=[bScr], writes=[bScr])
        P.add("vector", lambda v: v.reduce_max(scr[:, 10:11], scr[:, 24:32], axis=AXX),
              reads=[bScr], writes=[bScr])
        P.add("vector", lambda v: v.tensor_scalar(scr[:, 24:32], scr[:, 0:8], scr[:, 10:11],
                                                  None, ALU.is_ge),
              reads=[bScr], writes=[bScr])
        P.add("vector", lambda v: v.tensor_tensor(scr[:, 16:24], scr[:, 16:24], scr[:, 24:32],
                                                  ALU.mult),
              reads=[bScr], writes=[bScr])
        P.add("vector", lambda v: v.reduce_sum(scr[:, 11:12], scr[:, 16:24], axis=AXX),
              reads=[bScr], writes=[bScr])
        P.add("vector", lambda v: v.reciprocal(scr[:, 12:13], scr[:, 11:12]),
              reads=[bScr], writes=[bScr])
        P.add("vector", lambda v, tc=tc: v.tensor_scalar(
            WTM.s(tc * 8, (tc + 1) * 8), scr[:, 16:24], scr[:, 12:13], None, ALU.mult),
            reads=[bScr], writes=[bWTM])
    if "moew" in mk.dbg:
        apo = mk.dout("dbg_moew", [128, NTC * 8], F32)
        P.add("sync", lambda sy, apo=apo: sy.dma_start(apo, WTM.all()), reads=[bWTM], dma=1)

    W3s, bW3 = A.alloc("w3s", 16 * D, BF16)
    W12 = [A.alloc(f"w12_{i}", 8 * 256, BF16) for i in range(4)]
    Hb, bH = A.alloc("hbuf", 16 * 1024, BF16)

    for e in range(NEXP):
        P.add("gpsimd", lambda g, e=e: g.dma_start(
            W3s.all().rearrange("p (k j) -> p k j", k=16),
            w3_d[e].rearrange("(k p) j -> p k j", p=128)), writes=[bW3], dma=1)
        for half in range(2):
            hoff = half * 1024
            for q in range(8):
                (W1q, bW1q), (W2q, bW2q) = W12[2 * (q % 2)], W12[2 * (q % 2) + 1]
                P.add("gpsimd", lambda g, e=e, q=q, W1q=W1q: g.dma_start(
                    W1q.all().rearrange("p (k j) -> p k j", k=8),
                    w1_d[e][:, q * 256:(q + 1) * 256].rearrange("(k p) j -> p k j", p=128)),
                    writes=[bW1q], dma=1)
                P.add("gpsimd", lambda g, e=e, q=q, W2q=W2q: g.dma_start(
                    W2q.all().rearrange("p (k j) -> p k j", k=8),
                    w2_d[e][:, q * 256:(q + 1) * 256].rearrange("(k p) j -> p k j", p=128)),
                    writes=[bW2q], dma=1)
                for hc4 in range(2):
                    hc = 2 * q + hc4
                    for ts in range(2):
                        ka, kb = hc4 % 2, 2 + hc4 % 2

                        def fa(t, W1q=W1q, hc4=hc4, ts=ts, ka=ka, hoff=hoff):
                            for kc in range(NDC):
                                ins = t.matmul(PB[ka][:, :],
                                               W1q.s(kc * 256 + hc4 * 128, kc * 256 + (hc4 + 1) * 128),
                                               L2F.s(kc * T + hoff + ts * 512,
                                                     kc * T + hoff + (ts + 1) * 512),
                                               start=(kc == 0), stop=(kc == NDC - 1))
                            return ins
                        P.add("tensor", fa, reads=[bW1q, bL2F], writes=[bPB[ka]])

                        def fb(t, W2q=W2q, hc4=hc4, ts=ts, kb=kb, hoff=hoff):
                            for kc in range(NDC):
                                ins = t.matmul(PB[kb][:, :],
                                               W2q.s(kc * 256 + hc4 * 128, kc * 256 + (hc4 + 1) * 128),
                                               L2F.s(kc * T + hoff + ts * 512,
                                                     kc * T + hoff + (ts + 1) * 512),
                                               start=(kc == 0), stop=(kc == NDC - 1))
                            return ins
                        P.add("tensor", fb, reads=[bW2q, bL2F], writes=[bPB[kb]])

                        P.add("scalar", lambda s, ka=ka: s.activation(
                            scrb[:, 0:512], PB[ka][:, :], AF.Silu),
                            reads=[bPB[ka]], writes=[bScrb])
                        P.add("vector", lambda v, kb=kb, hc=hc, ts=ts: v.tensor_tensor(
                            Hb.s(hc * 1024 + ts * 512, hc * 1024 + (ts + 1) * 512),
                            scrb[:, 0:512], PB[kb][:, :], ALU.mult),
                            reads=[bScrb, bPB[kb]], writes=[bH])

            for tcl in range(8):
                tc = half * 8 + tcl
                for dh in range(2):
                    k = 4 + (tcl * 2 + dh) % 2

                    def fd(t, tcl=tcl, dh=dh, k=k):
                        for hc in range(16):
                            ins = t.matmul(PB[k][:, :],
                                           Hb.s(hc * 1024 + tcl * 128, hc * 1024 + (tcl + 1) * 128),
                                           W3s.s(hc * D + dh * 512, hc * D + (dh + 1) * 512),
                                           start=(hc == 0), stop=(hc == 15))
                        return ins
                    P.add("tensor", fd, reads=[bH, bW3], writes=[bPB[k]])
                    P.add("vector", lambda v, e=e, tc=tc, dh=dh, k=k: v.scalar_tensor_tensor(
                        XR[:, tc * D + dh * 512:tc * D + (dh + 1) * 512],
                        PB[k][:, :], WTM.s(tc * 8 + e, tc * 8 + e + 1),
                        XR[:, tc * D + dh * 512:tc * D + (dh + 1) * 512],
                        ALU.mult, ALU.add), reads=[bPB[k], bWTM], writes=[bXR])

    if "hb7" in mk.dbg:
        apo = mk.dout("dbg_hb7", [128, 16 * 1024], BF16)
        P.add("sync", lambda sy, apo=apo: sy.dma_start(apo, Hb.all()), reads=[bH], dma=1)
        apw = mk.dout("dbg_w37", [128, 16 * D], BF16)
        P.add("sync", lambda sy, apw=apw: sy.dma_start(apw, W3s.all()), reads=[bW3], dma=1)
    if "xr2" in mk.dbg:
        apo = mk.dout("dbg_xr2", [T, D], F32)
        P.add("sync", lambda sy, apo=apo: sy.dma_start(
            apo.rearrange("(c p) n -> p c n", p=128),
            XR[:, :].rearrange("p (c n) -> p c n", c=NTC)), reads=[bXR], dma=1)

    for vv, bb in ((L2F, bL2F), (WTM, bWTM), (W3s, bW3), (Hb, bH), (GB, bGB)):
        A.drop(vv, bb)
    for (vv, bb) in W12:
        A.drop(vv, bb)


_DFT_CACHE = {}


def _dft_tables():
    if "f" in _DFT_CACHE:
        return _DFT_CACHE["f"], _DFT_CACHE["i"]
    N = 4096
    t = np.arange(2048, dtype=np.float64)
    f = np.arange(2176, dtype=np.float64)
    ang = 2 * np.pi * np.outer(t, f) / N
    Fre = np.cos(ang)
    Fim = -np.sin(ang)
    Fre[:, 2049:] = 0.0
    Fim[:, 2049:] = 0.0
    w = np.full(2176, 2.0)
    w[0] = w[2048] = 1.0
    Ire = (w[None, :] / N) * np.cos(ang)
    Iim = -(w[None, :] / N) * np.sin(ang)
    Ire[:, 2049:] = 0.0
    Iim[:, 2049:] = 0.0
    cf = np.empty((17, 2, 16, 128, 128), np.float32)
    FreT = Fre.reshape(16, 128, 17, 128)
    FimT = Fim.reshape(16, 128, 17, 128)
    cf[:, 0] = FreT.transpose(2, 0, 1, 3)
    cf[:, 1] = FimT.transpose(2, 0, 1, 3)
    IreT = Ire.T.reshape(17, 128, 16, 128)
    IimT = Iim.T.reshape(17, 128, 16, 128)
    ci = np.empty((16, 17, 2, 128, 128), np.float32)
    ci[:, :, 0] = IreT.transpose(2, 0, 1, 3)
    ci[:, :, 1] = IimT.transpose(2, 0, 1, 3)
    _DFT_CACHE["f"] = cf
    _DFT_CACHE["i"] = ci
    return cf, ci


def _pack_cn(x):
    return np.ascontiguousarray(
        np.asarray(x, np.float32).reshape(512, 2, 64).transpose(1, 2, 0).reshape(128, 512))


def _pack_c(x):
    v = np.asarray(x, np.float32).reshape(512, 2).T
    return np.ascontiguousarray(np.broadcast_to(v[:, None, :], (2, 64, 512)).reshape(128, 512))


def _shard_inputs(inputs):
    f32 = lambda v: np.ascontiguousarray(np.asarray(v, np.float32))
    x = f32(inputs["L0_attn_x"])
    shared = dict(
        wq=f32(inputs["L0_ffn_wq"]), wk=f32(inputs["L0_ffn_wk"]),
        wv=f32(inputs["L0_ffn_wv"]), wo=f32(inputs["L0_ffn_wo"]),
        ln1s=f32(inputs["L0_ffn_ln1_scale"]).reshape(1, -1),
        ln1b=f32(inputs["L0_ffn_ln1_shift"]).reshape(1, -1),
        ln2s=f32(inputs["L0_ffn_ln2_scale"]).reshape(1, -1),
        ln2b=f32(inputs["L0_ffn_ln2_shift"]).reshape(1, -1),
        bo=f32(inputs["L0_ffn_bo"]).reshape(1, -1),
        gw=f32(inputs["L0_ffn_gate_w"]),
        w1=f32(inputs["L0_ffn_w1"]), w2=f32(inputs["L0_ffn_w2"]),
        w3=f32(inputs["L0_ffn_w3"]),
        ldt=_pack_c(inputs["L0_attn_log_dt"]),
        arl=_pack_cn(inputs["L0_attn_A_real_log"]),
        aim=_pack_cn(inputs["L0_attn_A_imag"]),
        cre=_pack_cn(inputs["L0_attn_C_real"]),
        cim=_pack_cn(inputs["L0_attn_C_imag"]),
        dvec=f32(inputs["L0_attn_D"]).reshape(1, -1),
        bout=f32(inputs["L0_attn_b_out"]).reshape(1, -1),
        wout=f32(inputs["L0_attn_W_out"]),
        wkl1=f32(inputs["L1_attn_W_K"]), wvl1=f32(inputs["L1_attn_W_V"]),
        wql1=f32(inputs["L1_attn_W_Q"]),
        mw0=f32(inputs["L1_attn_mem_W0"]), mw1=f32(inputs["L1_attn_mem_W1"]),
        knw=f32(inputs["L1_attn_k_norm_w"]).reshape(1, -1),
        qnw=f32(inputs["L1_attn_q_norm_w"]).reshape(1, -1),
        snw=f32(inputs["L1_attn_store_norm_w"]).reshape(-1),
        rnw=f32(inputs["L1_attn_retrieve_norm_w"]).reshape(-1),
        atw=np.ascontiguousarray(np.concatenate(
            [f32(inputs["L1_attn_alpha_w"]), f32(inputs["L1_attn_theta_w"])], axis=1)),
        atb=np.ascontiguousarray(np.concatenate(
            [f32(inputs["L1_attn_alpha_b"]).reshape(-1),
             f32(inputs["L1_attn_theta_b"]).reshape(-1)]).reshape(1, 32)),
        wgate=np.ascontiguousarray(np.asarray(inputs["L1_ffn_w_gate"], np.int32)),
        wup=np.ascontiguousarray(np.asarray(inputs["L1_ffn_w_up"], np.int32)),
        wdown=np.ascontiguousarray(np.asarray(inputs["L1_ffn_w_down"], np.int32)),
    )
    cf, ci = _dft_tables()
    shared["cdftf"] = cf
    shared["cdfti"] = ci
    maps = []
    for c in range(8):
        m = dict(shared)
        m["xin"] = x[c]
        maps.append(m)
    return maps


def kernel(**inputs):
    from concourse.bass_utils import run_bass_kernel_spmd
    maps = _shard_inputs(inputs)
    mk = build_core(dbg=(), upto=99)
    res = run_bass_kernel_spmd(mk.nc, maps, core_ids=list(range(8)))
    out = np.stack([np.asarray(res.results[c]["oout"], np.float32) for c in range(8)])
    return out


MAGIC = 12582912.0
INV2PI = float(1.0 / (2 * np.pi))
CW1 = float(np.float32(6.28125))
CW2 = float(np.float32(2 * np.pi - 6.28125))
CW3 = float(2 * np.pi - 6.28125 - float(np.float32(2 * np.pi - 6.28125)))


def emit_sincos(P, out_cos, out_sin, src, TMP, bufs, scr_bufs):
    bT = bufs
    N = 512

    def steps(shift, out_ap):
        P.add("vector", lambda v: v.tensor_scalar(TMP.s(0, N), src, INV2PI, shift,
                                                  ALU.mult, ALU.add),
              reads=scr_bufs, writes=[bT])
        P.add("vector", lambda v: v.tensor_scalar(TMP.s(0, N), TMP.s(0, N), MAGIC, MAGIC,
                                                  ALU.add, ALU.subtract),
              reads=[bT], writes=[bT])
        P.add("vector", lambda v: v.tensor_single_scalar(TMP.s(0, N), TMP.s(0, N), shift,
                                                         ALU.subtract),
              reads=[bT], writes=[bT])
        P.add("vector", lambda v: v.scalar_tensor_tensor(TMP.s(N, 2 * N), TMP.s(0, N), -CW1,
                                                         src, ALU.mult, ALU.add),
              reads=[bT] + list(scr_bufs), writes=[bT])
        P.add("vector", lambda v: v.scalar_tensor_tensor(TMP.s(N, 2 * N), TMP.s(0, N), -CW2,
                                                         TMP.s(N, 2 * N), ALU.mult, ALU.add),
              reads=[bT], writes=[bT])
        P.add("vector", lambda v: v.scalar_tensor_tensor(TMP.s(N, 2 * N), TMP.s(0, N), -CW3,
                                                         TMP.s(N, 2 * N), ALU.mult, ALU.add),
              reads=[bT], writes=[bT])
        P.add("scalar", lambda s: s.activation(out_ap, TMP.s(N, 2 * N), AF.Sin),
              reads=[bT], writes=list(scr_bufs))

    steps(0.0, out_sin)
    steps(0.25, out_cos)


def cplx_mul_block(P, dre, dim_, are, aim, bre, bim, TMP, bT, rbufs, wbufs, n):
    P.add("vector", lambda v: v.tensor_tensor(TMP.s(0, n), are, bre, ALU.mult),
          reads=rbufs, writes=[bT])
    P.add("vector", lambda v: v.tensor_tensor(TMP.s(n, 2 * n), aim, bim, ALU.mult),
          reads=rbufs, writes=[bT])
    P.add("vector", lambda v: v.tensor_tensor(dre, TMP.s(0, n), TMP.s(n, 2 * n), ALU.subtract),
          reads=[bT], writes=wbufs)
    P.add("vector", lambda v: v.tensor_tensor(TMP.s(0, n), are, bim, ALU.mult),
          reads=rbufs, writes=[bT])
    P.add("vector", lambda v: v.tensor_tensor(TMP.s(n, 2 * n), aim, bre, ALU.mult),
          reads=rbufs, writes=[bT])
    P.add("vector", lambda v: v.tensor_tensor(dim_, TMP.s(0, n), TMP.s(n, 2 * n), ALU.add),
          reads=[bT], writes=wbufs)


def l0_conv(P, mk, A, XR, bXR, stat, bStat, scr, bScr, scrb, bScrb, scrx, bScrx,
            ident, ones_bf, cvals, bConst, PB, bPB, PBT, bPBT,
            xnsp, krsp, uasp, bXnspD, bKrspD, bUaspD, ldt_d, arl_d, aim_d, cre_d, cim_d, dvec_d, bout_d,
            wout_d, cdftf_d, cdfti_d):
    NCH = 512

    PRM, bPRM = A.alloc("prm", 8 * NCH, F32)
    TMP, bTMP = A.alloc("cvtmp", 2 * 2048, F32)
    KQ = [A.alloc(f"kq{i}", 1024, F32R) for i in range(2)]
    print("P3.0 arena free:", A.free)
    kern_dbg = mk.dout("dbg_kern", [128, NTC * D], F32) if "kern" in mk.dbg else None

    SL_DT, SL_ARE, SL_AIM, SL_LRE, SL_LIM, SL_C2R, SL_C2I, SL_T = [i * NCH for i in range(8)]

    def ldprm(sy):
        return [sy.dma_start(PRM.s(SL_DT, SL_DT + NCH), ldt_d),
                sy.dma_start(PRM.s(SL_ARE, SL_ARE + NCH), arl_d),
                sy.dma_start(PRM.s(SL_AIM, SL_AIM + NCH), aim_d),
                sy.dma_start(PRM.s(SL_C2R, SL_C2R + NCH), cre_d),
                sy.dma_start(PRM.s(SL_C2I, SL_C2I + NCH), cim_d)]
    P.add("sync", ldprm, writes=[bPRM], dma=5)

    P.add("scalar", lambda s: s.activation(PRM.s(SL_DT, SL_DT + NCH), PRM.s(SL_DT, SL_DT + NCH), AF.Exp),
          reads=[bPRM], writes=[bPRM])
    P.add("scalar", lambda s: s.activation(PRM.s(SL_T, SL_T + NCH), PRM.s(SL_ARE, SL_ARE + NCH), AF.Exp),
          reads=[bPRM], writes=[bPRM])
    P.add("vector", lambda v: v.tensor_scalar_mul(PRM.s(SL_ARE, SL_ARE + NCH), PRM.s(SL_T, SL_T + NCH), -1.0),
          reads=[bPRM], writes=[bPRM])
    P.add("vector", lambda v: v.tensor_tensor(PRM.s(SL_LRE, SL_LRE + NCH), PRM.s(SL_DT, SL_DT + NCH),
                                              PRM.s(SL_ARE, SL_ARE + NCH), ALU.mult),
          reads=[bPRM], writes=[bPRM])
    P.add("vector", lambda v: v.tensor_tensor(PRM.s(SL_LIM, SL_LIM + NCH), PRM.s(SL_DT, SL_DT + NCH),
                                              PRM.s(SL_AIM, SL_AIM + NCH), ALU.mult),
          reads=[bPRM], writes=[bPRM])
    P.add("scalar", lambda s: s.activation(PRM.s(SL_T, SL_T + NCH), PRM.s(SL_LRE, SL_LRE + NCH), AF.Exp),
          reads=[bPRM], writes=[bPRM])
    emit_sincos(P, PRM.s(SL_DT, SL_DT + NCH), PRM.s(SL_LRE, SL_LRE + NCH),
                PRM.s(SL_LIM, SL_LIM + NCH), TMP, bTMP, [bPRM])
    P.add("vector", lambda v: v.tensor_tensor(PRM.s(SL_LIM, SL_LIM + NCH), PRM.s(SL_T, SL_T + NCH),
                                              PRM.s(SL_LRE, SL_LRE + NCH), ALU.mult),
          reads=[bPRM], writes=[bPRM])
    P.add("vector", lambda v: v.tensor_tensor(PRM.s(SL_LRE, SL_LRE + NCH), PRM.s(SL_T, SL_T + NCH),
                                              PRM.s(SL_DT, SL_DT + NCH), ALU.mult),
          reads=[bPRM], writes=[bPRM])
    P.add("vector", lambda v: v.tensor_single_scalar(PRM.s(SL_T, SL_T + NCH),
                                                     PRM.s(SL_LRE, SL_LRE + NCH), 1.0, ALU.subtract),
          reads=[bPRM], writes=[bPRM])
    P.add("vector", lambda v: v.tensor_tensor(TMP.s(0, NCH), PRM.s(SL_C2R, SL_C2R + NCH),
                                              PRM.s(SL_T, SL_T + NCH), ALU.mult),
          reads=[bPRM], writes=[bTMP])
    P.add("vector", lambda v: v.tensor_tensor(TMP.s(NCH, 2 * NCH), PRM.s(SL_C2I, SL_C2I + NCH),
                                              PRM.s(SL_LIM, SL_LIM + NCH), ALU.mult),
          reads=[bPRM], writes=[bTMP])
    P.add("vector", lambda v: v.tensor_tensor(TMP.s(2 * NCH, 3 * NCH), TMP.s(0, NCH),
                                              TMP.s(NCH, 2 * NCH), ALU.subtract),
          reads=[bTMP], writes=[bTMP])
    P.add("vector", lambda v: v.tensor_tensor(TMP.s(0, NCH), PRM.s(SL_C2R, SL_C2R + NCH),
                                              PRM.s(SL_LIM, SL_LIM + NCH), ALU.mult),
          reads=[bPRM], writes=[bTMP])
    P.add("vector", lambda v: v.tensor_tensor(TMP.s(NCH, 2 * NCH), PRM.s(SL_C2I, SL_C2I + NCH),
                                              PRM.s(SL_T, SL_T + NCH), ALU.mult),
          reads=[bPRM], writes=[bTMP])
    P.add("vector", lambda v: v.tensor_tensor(TMP.s(3 * NCH, 4 * NCH), TMP.s(0, NCH),
                                              TMP.s(NCH, 2 * NCH), ALU.add),
          reads=[bTMP], writes=[bTMP])
    P.add("vector", lambda v: v.tensor_tensor(TMP.s(0, NCH), PRM.s(SL_ARE, SL_ARE + NCH),
                                              PRM.s(SL_ARE, SL_ARE + NCH), ALU.mult),
          reads=[bPRM], writes=[bTMP])
    P.add("vector", lambda v: v.scalar_tensor_tensor(TMP.s(NCH, 2 * NCH), PRM.s(SL_AIM, SL_AIM + NCH),
                                                     1.0, PRM.s(SL_AIM, SL_AIM + NCH),
                                                     ALU.mult, ALU.mult),
          reads=[bPRM], writes=[bTMP])
    P.add("vector", lambda v: v.tensor_tensor(TMP.s(0, NCH), TMP.s(0, NCH), TMP.s(NCH, 2 * NCH),
                                              ALU.add),
          reads=[bTMP], writes=[bTMP])
    P.add("vector", lambda v: v.reciprocal(TMP.s(NCH, 2 * NCH), TMP.s(0, NCH)),
          reads=[bTMP], writes=[bTMP])
    P.add("vector", lambda v: v.tensor_scalar_mul(TMP.s(NCH, 2 * NCH), TMP.s(NCH, 2 * NCH), 2.0),
          reads=[bTMP], writes=[bTMP])
    P.add("vector", lambda v: v.tensor_tensor(TMP.s(0, NCH), TMP.s(2 * NCH, 3 * NCH),
                                              PRM.s(SL_ARE, SL_ARE + NCH), ALU.mult),
          reads=[bPRM, bTMP], writes=[bTMP])
    P.add("vector", lambda v: v.tensor_tensor(PRM.s(SL_T, SL_T + NCH), TMP.s(3 * NCH, 4 * NCH),
                                              PRM.s(SL_AIM, SL_AIM + NCH), ALU.mult),
          reads=[bPRM, bTMP], writes=[bPRM])
    P.add("vector", lambda v: v.tensor_tensor(TMP.s(0, NCH), TMP.s(0, NCH),
                                              PRM.s(SL_T, SL_T + NCH), ALU.add),
          reads=[bPRM, bTMP], writes=[bTMP])
    P.add("vector", lambda v: v.tensor_tensor(PRM.s(SL_T, SL_T + NCH), TMP.s(3 * NCH, 4 * NCH),
                                              PRM.s(SL_ARE, SL_ARE + NCH), ALU.mult),
          reads=[bPRM, bTMP], writes=[bPRM])
    P.add("vector", lambda v: v.tensor_tensor(TMP.s(3 * NCH, 4 * NCH), TMP.s(2 * NCH, 3 * NCH),
                                              PRM.s(SL_AIM, SL_AIM + NCH), ALU.mult),
          reads=[bPRM, bTMP], writes=[bTMP])
    P.add("vector", lambda v: v.tensor_tensor(PRM.s(SL_C2I, SL_C2I + NCH), PRM.s(SL_T, SL_T + NCH),
                                              TMP.s(3 * NCH, 4 * NCH), ALU.subtract),
          reads=[bPRM, bTMP], writes=[bPRM])
    P.add("vector", lambda v: v.tensor_tensor(PRM.s(SL_C2R, SL_C2R + NCH), TMP.s(0, NCH),
                                              TMP.s(NCH, 2 * NCH), ALU.mult),
          reads=[bTMP], writes=[bPRM])
    P.add("vector", lambda v: v.tensor_tensor(PRM.s(SL_C2I, SL_C2I + NCH), PRM.s(SL_C2I, SL_C2I + NCH),
                                              TMP.s(NCH, 2 * NCH), ALU.mult),
          reads=[bTMP], writes=[bPRM])

    if "prm" in mk.dbg:
        apo = mk.dout("dbg_prm", [128, 8 * NCH])
        P.add("sync", lambda sy, apo=apo: sy.dma_start(apo, PRM.s(0, 8 * NCH)), reads=[bPRM], dma=1)

    _offs = {}

    def _chk(name, v):
        if name in _offs:
            assert _offs[name] == v.off, (name, _offs[name], v.off)
        _offs[name] = v.off
        return v

    for hcx in range(16):
        c0 = 32 * hcx
        LP, bLP = A.alloc("lp", 2 * 12 * 32, F32); _chk("lp", LP)

        def lps(i, comp):
            return LP.s((comp * 12 + i) * 32, (comp * 12 + i) * 32 + 32)
        P.add("vector", lambda v, c0=c0: v.tensor_copy(lps(0, 0), PRM.s(SL_LRE + c0, SL_LRE + c0 + 32)),
              reads=[bPRM], writes=[bLP])
        P.add("vector", lambda v, c0=c0: v.tensor_copy(lps(0, 1), PRM.s(SL_LIM + c0, SL_LIM + c0 + 32)),
              reads=[bPRM], writes=[bLP])
        for sidx in range(1, 11):
            P.add("vector", lambda v, sidx=sidx: v.tensor_tensor(
                TMP.s(0, 32), lps(sidx - 1, 0), lps(sidx - 1, 0), ALU.mult),
                reads=[bLP], writes=[bTMP])
            P.add("vector", lambda v, sidx=sidx: v.tensor_tensor(
                TMP.s(32, 64), lps(sidx - 1, 1), lps(sidx - 1, 1), ALU.mult),
                reads=[bLP], writes=[bTMP])
            P.add("vector", lambda v, sidx=sidx: v.tensor_tensor(
                lps(sidx, 0), TMP.s(0, 32), TMP.s(32, 64), ALU.subtract),
                reads=[bTMP], writes=[bLP])
            P.add("vector", lambda v, sidx=sidx: v.tensor_tensor(
                TMP.s(0, 32), lps(sidx - 1, 0), lps(sidx - 1, 1), ALU.mult),
                reads=[bLP], writes=[bTMP])
            P.add("vector", lambda v, sidx=sidx: v.tensor_scalar_mul(
                lps(sidx, 1), TMP.s(0, 32), 2.0),
                reads=[bTMP], writes=[bLP])

        MRE, bMRE = A.alloc("mre", 512, F32); _chk("mre", MRE)
        MIM, bMIM = A.alloc("mim", 512, F32); _chk("mim", MIM)

        def mview(V):
            return V.s(0, 512).rearrange("p (pr q) -> p pr q", pr=32)
        P.add("gpsimd", lambda g: g.memset(mview(MRE)[:, :, 0:1], 1.0), writes=[bMRE])
        P.add("gpsimd", lambda g: g.memset(mview(MIM)[:, :, 0:1], 0.0), writes=[bMIM])
        for sx in range(4):
            blk = 2 ** sx
            lre_b = lps(7 + sx, 0).unsqueeze(2).broadcast_to((128, 32, blk))
            lim_b = lps(7 + sx, 1).unsqueeze(2).broadcast_to((128, 32, blk))
            cplx_mul_block(P, mview(MRE)[:, :, blk:2 * blk], mview(MIM)[:, :, blk:2 * blk],
                           mview(MRE)[:, :, 0:blk], mview(MIM)[:, :, 0:blk],
                           lre_b, lim_b, TMP, bTMP, [bMRE, bMIM, bLP], [bMRE, bMIM],
                           32 * blk)
        MBR, bMBR = A.alloc("mbr", 2 * 512, F32R); _chk("mbr", MBR)
        MBI, bMBI = A.alloc("mbi", 2 * 512, F32R); _chk("mbi", MBI)
        for V, b, MB, bMB in ((MRE, bMRE, MBR, bMBR), (MIM, bMIM, MBI, bMBI)):
            P.add("vector", lambda v, MB=MB: v.tensor_copy(
                MB.all(), cvals[:, 3:4].broadcast_to((128, 1024))),
                reads=[bConst], writes=[bMB])
            P.add("vector", lambda v, V=V, MB=MB: v.tensor_copy(
                MB.s(0, 512)[0:64, :], V.s(0, 512)[0:64, :]),
                reads=[b], writes=[bMB])
            P.add("vector", lambda v, V=V, MB=MB: v.tensor_copy(
                MB.s(512, 1024)[64:128, :], V.s(0, 512)[64:128, :]),
                reads=[b], writes=[bMB])
        A.drop(MRE, bMRE)
        A.drop(MIM, bMIM)

        GRE, bGRE = A.alloc("gre", 4096, F32); _chk("gre", GRE)
        GIM, bGIM = A.alloc("gim", 4096, F32); _chk("gim", GIM)

        def gview(V):
            return V.s(0, 4096).rearrange("p (pr r) -> p pr r", pr=32)
        P.add("vector", lambda v, c0=c0: v.tensor_copy(
            gview(GRE)[:, :, 0:1], PRM.s(SL_C2R + c0, SL_C2R + c0 + 32).unsqueeze(2)),
            reads=[bPRM], writes=[bGRE])
        P.add("vector", lambda v, c0=c0: v.tensor_copy(
            gview(GIM)[:, :, 0:1], PRM.s(SL_C2I + c0, SL_C2I + c0 + 32).unsqueeze(2)),
            reads=[bPRM], writes=[bGIM])
        for sx in range(7):
            blk = 2 ** sx
            lre_b = lps(sx, 0).unsqueeze(2).broadcast_to((128, 32, blk))
            lim_b = lps(sx, 1).unsqueeze(2).broadcast_to((128, 32, blk))
            cplx_mul_block(P, gview(GRE)[:, :, blk:2 * blk], gview(GIM)[:, :, blk:2 * blk],
                           gview(GRE)[:, :, 0:blk], gview(GIM)[:, :, 0:blk],
                           lre_b, lim_b, TMP, bTMP, [bGRE, bGIM, bLP], [bGRE, bGIM],
                           32 * blk)
        GIN, bGIN = A.alloc("gin", 4096, F32R); _chk("gin", GIN)
        P.add("scalar", lambda s: s.activation(GIN.s(0, 4096), GIM.s(0, 4096), AF.Identity,
                                               scale=-1.0),
              reads=[bGIM], writes=[bGIN])
        A.drop(GIM, bGIM)
        GR2, bGR2 = A.alloc("gr2", 4096, F32R); _chk("gr2", GR2)
        P.add("scalar", lambda s: s.activation(GR2.s(0, 4096), GRE.s(0, 4096), AF.Identity),
              reads=[bGRE], writes=[bGR2])
        A.drop(GRE, bGRE)

        KQv, bKQv = KQ[hcx % 2]
        for grp in range(2):
            pk = 4 + grp % 2
            for ci in range(32):
                cl = 32 * grp + ci
                pair, par = cl // 2, cl % 2

                def fkm(t, pair=pair, par=par, ci=ci, first=(ci == 0), pk=pk):
                    mo = 512 * par
                    t.matmul(PB[pk][:, ci * 16:(ci + 1) * 16],
                             GR2.sr(pair * 128, (pair + 1) * 128),
                             MBR.sr(mo + pair * 16, mo + (pair + 1) * 16),
                             start=first, stop=False)
                    return t.matmul(PB[pk][:, ci * 16:(ci + 1) * 16],
                                    GIN.sr(pair * 128, (pair + 1) * 128),
                                    MBI.sr(mo + pair * 16, mo + (pair + 1) * 16),
                                    start=False, stop=(ci == 31))
                P.add("tensor", fkm, reads=[bGR2, bGIN, bMBR, bMBI], writes=[bPB[pk]])

            def fke(v, grp=grp, pk=pk, KQv=KQv):
                return v.tensor_copy(
                    KQv.s(0, 1024).rearrange("p (lc c) -> p lc c", lc=16)
                    [:, :, 32 * grp:32 * grp + 32].transpose([0, 2, 1]),
                    PB[pk][:, :].rearrange("p (c lc) -> p c lc", c=32))
            P.add("vector", fke, reads=[bPB[pk]], writes=[bKQv])

        P.add("sync", lambda sy, hcx=hcx, KQv=KQv: sy.dma_start(
            krsp.rearrange("p (lc c) -> p lc c", lc=16)[:, :, 64 * hcx:64 * hcx + 64],
            KQv.s(0, 1024).rearrange("p (lc c) -> p lc c", lc=16)), reads=[bKQv],
            writes=[bKrspD], dma=1)
        if kern_dbg is not None:
            P.add("gpsimd", lambda g, hcx=hcx, KQv=KQv: g.dma_start(
                kern_dbg.rearrange("p (lc c) -> p lc c", lc=16)[:, :, 64 * hcx:64 * hcx + 64],
                KQv.s(0, 1024).rearrange("p (lc c) -> p lc c", lc=16)), reads=[bKQv], dma=1)
        A.drop(GR2, bGR2)
        A.drop(GIN, bGIN)
        A.drop(MBR, bMBR)
        A.drop(MBI, bMBI)
        A.drop(LP, bLP)

    for (vv, bb) in KQ:
        A.drop(vv, bb)
    A.drop(PRM, bPRM)
    A.drop(TMP, bTMP)

    QW = 256
    DB, bDB = A.alloc("db", D, BF16)
    P.add("gpsimd", lambda g: g.dma_start(scrb[0:1, 0:D], dvec_d),
          writes=[bScrb], dma=1)
    for hf in range(2):
        P.add("tensor", lambda t, hf=hf: t.matmul(
            PB[0][:, :], ones_bf[0:1, 0:128], scrb[0:1, hf * 512:(hf + 1) * 512],
            start=True, stop=True), reads=[bScrb, bConst], writes=[bPB[0]])
        P.add("vector", lambda v, hf=hf: v.tensor_copy(DB.s(hf * 512, (hf + 1) * 512),
                                                       PB[0][:, :]),
              reads=[bPB[0]], writes=[bDB])

    for qq in range(4):
        XNQ, bXNQ = A.alloc("xnq", NTC * QW, F32R); _chk("xnq", XNQ)
        P.add("gpsimd", lambda g, qq=qq: g.dma_start(
            XNQ.s(0, NTC * QW).rearrange("p (c n) -> p c n", c=NTC),
            xnsp.rearrange("p (c n) -> p c n", c=NTC)[:, :, qq * QW:(qq + 1) * QW]),
            reads=[bXnspD], writes=[bXNQ], dma=1)
        KRQ, bKRQ = A.alloc("krq", NTC * QW, F32R); _chk("krq", KRQ)
        P.add("sync", lambda sy, qq=qq: sy.dma_start(
            KRQ.s(0, NTC * QW).rearrange("p (lc c) -> p lc c", lc=16),
            krsp.rearrange("p (lc c) -> p lc c", lc=16)[:, :, qq * QW:(qq + 1) * QW]),
            reads=[bKrspD], writes=[bKRQ], dma=1)
        VBUF, bVBUF = A.alloc("vbuf", 17 * 2 * QW, F32R); _chk("vbuf", VBUF)
        FB = [A.alloc(f"fb{i}", 4096, F32R) for i in range(2)]
        _chk("fb0", FB[0][0]); _chk("fb1", FB[1][0])
        CT, bCT = A.alloc("ctmp", 2 * QW, F32); _chk("ctmp", CT)

        for ft in range(17):
            FBv, bFBv = FB[ft % 2]
            P.add("sync" if ft % 2 == 0 else "gpsimd",
                  lambda e, ft=ft, FBv=FBv: e.dma_start(
                      FBv.s(0, 4096).rearrange("p (a f) -> p a f", a=32),
                      cdftf_d[ft].rearrange("a b p f -> p (a b) f")),
                  writes=[bFBv], dma=1)

            def fp1(t, ft=ft, FBv=FBv):
                ins = None
                for tcc in range(NTC):
                    for ri in range(2):
                        lh = FBv.sr((ri * 16 + tcc) * 128, (ri * 16 + tcc) * 128 + 128)
                        t.matmul(PB[2 * ri][:, 0:QW], lh,
                                 XNQ.sr(tcc * QW, (tcc + 1) * QW),
                                 start=(tcc == 0), stop=(tcc == NTC - 1))
                        ins = t.matmul(PB[2 * ri + 1][:, 0:QW], lh,
                                       KRQ.sr(tcc * QW, (tcc + 1) * QW),
                                       start=(tcc == 0), stop=(tcc == NTC - 1))
                return ins
            P.add("tensor", fp1, reads=[bFBv, bXNQ, bKRQ],
                  writes=[bPB[0], bPB[1], bPB[2], bPB[3]])

            P.add("scalar", lambda s: s.activation(scr[:, 0:QW], PB[0][:, 0:QW], AF.Identity),
                  reads=[bPB[0]], writes=[bScr])
            P.add("scalar", lambda s: s.activation(scr[:, QW:2 * QW], PB[2][:, 0:QW], AF.Identity),
                  reads=[bPB[2]], writes=[bScr])
            P.add("vector", lambda v: v.tensor_tensor(CT.s(0, QW), scr[:, 0:QW], PB[1][:, 0:QW],
                                                      ALU.mult), reads=[bScr, bPB[1]], writes=[bCT])
            P.add("vector", lambda v: v.tensor_tensor(CT.s(QW, 2 * QW), scr[:, QW:2 * QW],
                                                      PB[3][:, 0:QW],
                                                      ALU.mult), reads=[bScr, bPB[3]], writes=[bCT])
            P.add("vector", lambda v, ft=ft: v.tensor_tensor(
                VBUF.s(ft * 2 * QW, ft * 2 * QW + QW), CT.s(0, QW), CT.s(QW, 2 * QW), ALU.subtract),
                reads=[bCT], writes=[bVBUF])
            P.add("vector", lambda v: v.tensor_tensor(CT.s(0, QW), scr[:, 0:QW], PB[3][:, 0:QW],
                                                      ALU.mult), reads=[bScr, bPB[3]], writes=[bCT])
            P.add("vector", lambda v: v.tensor_tensor(CT.s(QW, 2 * QW), scr[:, QW:2 * QW],
                                                      PB[1][:, 0:QW],
                                                      ALU.mult), reads=[bScr, bPB[1]], writes=[bCT])
            P.add("vector", lambda v, ft=ft: v.tensor_tensor(
                VBUF.s(ft * 2 * QW + QW, ft * 2 * QW + 2 * QW), CT.s(0, QW), CT.s(QW, 2 * QW),
                ALU.add),
                reads=[bCT], writes=[bVBUF])

        for (vv, bb) in FB:
            A.drop(vv, bb)
        A.drop(KRQ, bKRQ)
        A.drop(CT, bCT)

        IB = [A.alloc(f"ib{i}", 4352, F32R) for i in range(2)]
        _chk("ib0", IB[0][0]); _chk("ib1", IB[1][0])
        UAH, bUAH = A.alloc("uah", NTC * QW, BF16); _chk("uah", UAH)
        for tc in range(NTC):
            IBv, bIBv = IB[tc % 2]
            P.add("sync" if tc % 2 == 0 else "gpsimd",
                  lambda e, tc=tc, IBv=IBv: e.dma_start(
                      IBv.s(0, 4352).rearrange("p (a t) -> p a t", a=34),
                      cdfti_d[tc].rearrange("a b p t -> p (a b) t")),
                  writes=[bIBv], dma=1)
            pk = 4 + tc % 2

            def fp2(t, IBv=IBv, pk=pk):
                ins = None
                for ft in range(17):
                    for ri in range(2):
                        ins = t.matmul(PB[pk][:, 0:QW],
                                       IBv.sr((ft * 2 + ri) * 128, (ft * 2 + ri) * 128 + 128),
                                       VBUF.sr(ft * 2 * QW + ri * QW, ft * 2 * QW + (ri + 1) * QW),
                                       start=(ft == 0 and ri == 0), stop=(ft == 16 and ri == 1))
                return ins
            P.add("tensor", fp2, reads=[bIBv, bVBUF], writes=[bPB[pk]])
            P.add("vector", lambda v, tc=tc, qq=qq: v.tensor_tensor(
                scr[:, 0:QW], XNQ.s(tc * QW, (tc + 1) * QW),
                DB.s(qq * QW, (qq + 1) * QW), ALU.mult),
                reads=[bXNQ, bDB], writes=[bScr])
            P.add("vector", lambda v, pk=pk: v.tensor_tensor(
                scr[:, QW:2 * QW], PB[pk][:, 0:QW], scr[:, 0:QW], ALU.add),
                reads=[bPB[pk], bScr], writes=[bScr])
            P.add("scalar", lambda s, tc=tc: s.activation(
                UAH.s(tc * QW, (tc + 1) * QW), scr[:, QW:2 * QW],
                AF.Gelu_apprx_tanh), reads=[bScr], writes=[bUAH])

        P.add("sync", lambda sy, qq=qq: sy.dma_start(
            uasp.rearrange("p (c n) -> p c n", c=NTC)[:, :, qq * QW:(qq + 1) * QW],
            UAH.all().rearrange("p (c n) -> p c n", c=NTC)), reads=[bUAH],
            writes=[bUaspD], dma=1)
        for (vv, bb) in IB:
            A.drop(vv, bb)
        A.drop(VBUF, bVBUF)
        A.drop(UAH, bUAH)
        A.drop(XNQ, bXNQ)

    A.drop(DB, bDB)

    WOUT, bWOUT = A.alloc("wout", 8 * 2048, BF16)
    P.add("gpsimd", lambda g: g.dma_start(
        WOUT.all().rearrange("p (k j) -> p k j", k=8),
        wout_d.rearrange("(k p) j -> p k j", p=128)), writes=[bWOUT], dma=1)
    BR, bBR = A.alloc("br", 2048, BF16)
    P.add("gpsimd", lambda g: g.dma_start(BR.s(0, 2048)[0:1, :], bout_d),
          writes=[bBR], dma=1)
    UF, bUF = A.alloc("uf", 1024, BF16)
    UB = [A.alloc(f"ub{i}", 1024, BF16) for i in range(2)]
    for tc in range(NTC):
        UBv, bUBv = UB[tc % 2]
        P.add("sync", lambda sy, tc=tc, UBv=UBv: sy.dma_start(
            UBv.all(), uasp.rearrange("p (c n) -> p c n", c=NTC)[:, tc, :]),
            reads=[bUaspD], writes=[bUBv], dma=1)
        for c in range(NDC):
            k = c % 2
            P.add("tensor", lambda t, UBv=UBv, c=c, k=k: t.transpose(
                PBT[k][:, 0:128], UBv.s(c * 128, (c + 1) * 128),
                ident[:, :]), reads=[bUBv, bConst], writes=[bPBT[k]])
            P.add("vector", lambda v, c=c, k=k: v.tensor_copy(
                UF.s(c * 128, (c + 1) * 128), PBT[k][:, 0:128]),
                reads=[bPBT[k]], writes=[bUF])
        for os in range(4):
            def fz(t, os=os):
                for kc in range(NDC):
                    t.matmul(PB[os][:, :], UF.s(kc * 128, (kc + 1) * 128),
                             WOUT.s(kc * 2048 + os * 512, kc * 2048 + (os + 1) * 512),
                             start=(kc == 0), stop=False)
                return t.matmul(PB[os][:, :], ones_bf[0:1, 0:128],
                                BR.s(os * 512, (os + 1) * 512)[0:1, :],
                                start=False, stop=True)
            P.add("tensor", fz, reads=[bUF, bWOUT, bBR, bConst], writes=[bPB[os]])
        for half in range(2):
            P.add("scalar", lambda s, half=half: s.activation(
                scrb[:, half * 512:(half + 1) * 512], PB[2 + half][:, :], AF.Sigmoid),
                reads=[bPB[2 + half]], writes=[bScrb])
            P.add("vector", lambda v, half=half: v.tensor_tensor(
                scr[:, 0:512], PB[half][:, :], scrb[:, half * 512:(half + 1) * 512], ALU.mult),
                reads=[bPB[half], bScrb], writes=[bScr])
            P.add("vector", lambda v, tc=tc, half=half: v.tensor_tensor(
                XR[:, tc * D + half * 512:tc * D + (half + 1) * 512],
                XR[:, tc * D + half * 512:tc * D + (half + 1) * 512],
                scr[:, 0:512], ALU.add), reads=[bScr], writes=[bXR])

    A.drop(WOUT, bWOUT)
    A.drop(UF, bUF)
    A.drop(BR, bBR)
    for (vv, bb) in UB:
        A.drop(vv, bb)
    if "xr3" in mk.dbg:
        apo = mk.dout("dbg_xr3", [T, D], F32)
        P.add("sync", lambda sy, apo=apo: sy.dma_start(
            apo.rearrange("(c p) n -> p c n", p=128),
            XR[:, :].rearrange("p (c n) -> p c n", c=NTC)), reads=[bXR], dma=1)


AT_ON = True
KC_ON = 5


def l1_attn(P, mk, A, XR, bXR, stat, bStat, st0, bSt0, scr, bScr, scrx, bScrx,
            scrb, bScrb, ident, ones_bf, cvals, bConst, ONER, OROW, PB, bPB,
            PBT, bPBT, xnsp, bXnspD, wk_d, wv_d, wq_d, mw0_d, mw1_d,
            knw_d, qnw_d, snw_d, rnw_d, atw_d, atb_d, upto):
    HD, NH = 64, 16
    _offs = {}

    def _chk(name, v):
        if name in _offs:
            assert _offs[name] == v.off, (name, _offs[name], v.off)
        _offs[name] = v.off
        return v

    ln_compute_stats(P, lambda P_, tc: (XR[:, tc * D:(tc + 1) * D], [bXR]),
                     stat, bStat, st0, bSt0, scr, bScr, cvals[:, 0:1], bConst)
    P.add("vector", lambda v: v.tensor_tensor(stat[:, 0:16], st0[:, 0:16], st0[:, 0:16],
                                              ALU.mult), reads=[bSt0], writes=[bStat])
    P.add("vector", lambda v: v.tensor_tensor(stat[:, 0:16], stat[:, 0:16], stat[:, 32:48],
                                              ALU.mult), reads=[bStat], writes=[bStat])
    P.add("scalar", lambda s: s.activation(stat[:, 16:32], stat[:, 0:16], AF.Sqrt,
                                           bias=cvals[:, 1:2]),
          reads=[bStat, bConst], writes=[bStat])
    P.add("vector", lambda v: v.reciprocal(stat[:, 48:64], stat[:, 16:32]),
          reads=[bStat], writes=[bStat])

    def srs(tc):
        return stat[:, 48 + tc:49 + tc]

    VT, bVT = A.alloc("l1vt", NTC * D, BF16)
    KT, bKT = A.alloc("l1kt", NTC * D, BF16)
    THETA, bTHETA = A.alloc("l1theta", NTC * 16, F32)
    KNB, bKNB = A.alloc("l1knb", D, BF16)
    ATB, bATB = A.alloc("l1atb", 32, F32)
    SW, bSW = A.alloc("l1sw", 8, F32)
    AMB, bAMB = A.alloc("l1amb", 16, F32)
    AMACC, bAMACC = A.alloc("l1amacc", 16, F32)

    P.add("sync", lambda sy: sy.dma_start(SW.s(0, 8), snw_d.rearrange("(c p) -> p c", p=128),
                                          allow_slow_non_contiguous=True),
          writes=[bSW], dma=1)
    P.add("gpsimd", lambda g: g.dma_start(scrb[0:1, 0:64], knw_d), writes=[bScrb], dma=1)
    P.add("tensor", lambda t: t.matmul(PB[0][:, 0:64], ones_bf[0:1, 0:128], scrb[0:1, 0:64],
                                       start=True, stop=True),
          reads=[bScrb, bConst], writes=[bPB[0]])
    P.add("vector", lambda v: v.tensor_copy(
        KNB.s(0, D).rearrange("p (h d) -> p h d", h=16),
        PB[0][:, 0:64].unsqueeze(1).broadcast_to((128, 16, 64))),
        reads=[bPB[0]], writes=[bKNB])
    P.add("gpsimd", lambda g: g.dma_start(scrb[0:1, 64:96], atb_d), writes=[bScrb], dma=1)
    P.add("tensor", lambda t: t.matmul(PB[0][:, 0:32], ones_bf[0:1, 0:128], scrb[0:1, 64:96],
                                       start=True, stop=True),
          reads=[bScrb, bConst], writes=[bPB[0]])
    P.add("vector", lambda v: v.tensor_copy(ATB.s(0, 32), PB[0][:, 0:32]),
          reads=[bPB[0]], writes=[bATB])

    WKC, bWKC = A.alloc("l1wkc", 8 * D, BF16)
    WVC, bWVC = A.alloc("l1wvc", 8 * D, BF16)
    ATW, bATW = A.alloc("l1atw", 8 * 32, BF16)
    for dc in range(8):
        P.add("gpsimd", lambda g, dc=dc: g.dma_start(
            WKC.s(dc * D, (dc + 1) * D), wk_d[dc * 128:(dc + 1) * 128, :]),
            writes=[bWKC], dma=1)
        P.add("vector", lambda v, dc=dc: v.tensor_scalar(
            WKC.s(dc * D, (dc + 1) * D), WKC.s(dc * D, (dc + 1) * D),
            SW.s(dc, dc + 1), 1.0, ALU.mult, ALU.mult),
            reads=[bWKC, bSW], writes=[bWKC])
        P.add("gpsimd", lambda g, dc=dc: g.dma_start(
            WVC.s(dc * D, (dc + 1) * D), wv_d[dc * 128:(dc + 1) * 128, :]),
            writes=[bWVC], dma=1)
        P.add("vector", lambda v, dc=dc: v.tensor_scalar(
            WVC.s(dc * D, (dc + 1) * D), WVC.s(dc * D, (dc + 1) * D),
            SW.s(dc, dc + 1), 1.0, ALU.mult, ALU.mult),
            reads=[bWVC, bSW], writes=[bWVC])
        P.add("gpsimd", lambda g, dc=dc: g.dma_start(
            ATW.s(dc * 32, (dc + 1) * 32), atw_d[dc * 128:(dc + 1) * 128, :]),
            writes=[bATW], dma=1)
        P.add("vector", lambda v, dc=dc: v.tensor_scalar(
            ATW.s(dc * 32, (dc + 1) * 32), ATW.s(dc * 32, (dc + 1) * 32),
            SW.s(dc, dc + 1), 1.0, ALU.mult, ALU.mult),
            reads=[bATW, bSW], writes=[bATW])

    if "wvdump" in mk.dbg:
        awk = mk.dout("dbg_wkdump", [128, 1024], BF16)
        awv = mk.dout("dbg_wvdump", [128, 1024], BF16)
        P.add("sync", lambda sy: sy.dma_start(awk, WKC.s(0, D)), reads=[bWKC], dma=1)
        P.add("sync", lambda sy: sy.dma_start(awv, WVC.s(0, D)), reads=[bWVC], dma=1)
    XNFt = [A.alloc(f"l1xnf{i}", 8 * 128, BF16) for i in range(2)]
    SQT, bSQT = A.alloc("l1sqt", D, F32)
    ALB, bALB = A.alloc("l1alb", 16, BF16)

    for tc in range(NTC):
        XNv, bXNv = XNFt[tc % 2]
        P.add("vector", lambda v, tc=tc: v.tensor_scalar(
            scrx[:, 0:D], XR[:, tc * D:(tc + 1) * D],
            st0[:, 16 + tc:17 + tc], st0[:, tc:tc + 1], ALU.subtract, ALU.mult),
            reads=[bXR, bSt0], writes=[bScrx])
        P.add("sync", lambda sy, tc=tc: sy.dma_start(
            xnsp.rearrange("p (c n) -> p c n", c=NTC)[:, tc, :], scrx[:, 0:D]),
            reads=[bScrx], writes=[bXnspD], dma=1)
        for dc in range(8):
            P.add("tensor", lambda t, dc=dc: t.transpose(
                PBT[dc % 2][:, 0:128], scrx[:, dc * 128:(dc + 1) * 128], ident[:, :]),
                reads=[bScrx, bConst], writes=[bPBT[dc % 2]])
            P.add("vector", lambda v, dc=dc, XNv=XNv: v.tensor_copy(
                XNv.s(dc * 128, (dc + 1) * 128), PBT[dc % 2][:, 0:128]),
                reads=[bPBT[dc % 2]], writes=[bXNv])

        def fproj(t, XNv=XNv):
            for dc in range(8):
                lh = XNv.s(dc * 128, (dc + 1) * 128)
                st, sp = (dc == 0), (dc == 7)
                t.matmul(PB[0][:, :], lh, WVC.s(dc * D, dc * D + 512), start=st, stop=sp)
                t.matmul(PB[1][:, :], lh, WVC.s(dc * D + 512, (dc + 1) * D), start=st, stop=sp)
                t.matmul(PB[2][:, :], lh, WKC.s(dc * D, dc * D + 512), start=st, stop=sp)
                ins = t.matmul(PB[3][:, :], lh, WKC.s(dc * D + 512, (dc + 1) * D),
                               start=st, stop=sp)
            if AT_ON:
                for dc in range(8):
                    ins = t.matmul(PB[4][:, 0:32], XNv.s(dc * 128, (dc + 1) * 128),
                                   ATW.s(dc * 32, (dc + 1) * 32),
                                   start=(dc == 0), stop=(dc == 7))
            return ins
        P.add("tensor", fproj, reads=[bXNv, bWKC, bWVC, bATW],
              writes=[bPB[0], bPB[1], bPB[2], bPB[3], bPB[4]])

        P.add("vector", lambda v, tc=tc: v.tensor_scalar(
            VT.s(tc * D, tc * D + 512), PB[0][:, :], srs(tc), 1.0, ALU.mult, ALU.mult),
            reads=[bPB[0], bStat], writes=[bVT])
        P.add("vector", lambda v, tc=tc: v.tensor_scalar(
            VT.s(tc * D + 512, (tc + 1) * D), PB[1][:, :], srs(tc), 1.0, ALU.mult, ALU.mult),
            reads=[bPB[1], bStat], writes=[bVT])
        if KC_ON == 0:
            P.add("vector", lambda v, tc=tc: v.tensor_copy(
                KT.s(tc * D, tc * D + 512), PB[2][:, :]),
                reads=[bPB[2]], writes=[bKT])
            continue
        P.add("scalar", lambda s: s.activation(scr[:, 0:512], PB[2][:, :], AF.Identity),
              reads=[bPB[2]], writes=[bScr])
        P.add("scalar", lambda s: s.activation(scr[:, 512:1024], PB[3][:, :], AF.Identity),
              reads=[bPB[3]], writes=[bScr])
        if KC_ON >= 2:
            P.add("vector", lambda v: v.scalar_tensor_tensor(
                SQT.s(0, D), scr[:, 0:D], 1.0, scr[:, 0:D], ALU.mult, ALU.mult),
                reads=[bScr], writes=[bSQT])
            P.add("vector", lambda v: v.reduce_sum(
                stat[:, 0:16].unsqueeze(2), SQT.s(0, D).rearrange("p (h d) -> p h d", h=16),
                axis=AXX), reads=[bSQT], writes=[bStat])
        if KC_ON >= 3:
            P.add("vector", lambda v, tc=tc: v.tensor_tensor(
                stat[:, 16:17], srs(tc), srs(tc), ALU.mult), reads=[bStat], writes=[bStat])
            P.add("vector", lambda v: v.tensor_scalar(
                stat[:, 0:16], stat[:, 0:16], stat[:, 16:17], 1.0 / HD, ALU.mult, ALU.mult),
                reads=[bStat], writes=[bStat])
            P.add("scalar", lambda s: s.activation(stat[:, 16:32], stat[:, 0:16], AF.Sqrt,
                                                   bias=cvals[:, 1:2]),
                  reads=[bStat, bConst], writes=[bStat])
            P.add("vector", lambda v: v.reciprocal(stat[:, 0:16], stat[:, 16:32]),
                  reads=[bStat], writes=[bStat])
            P.add("vector", lambda v, tc=tc: v.tensor_scalar(
                stat[:, 16:32], stat[:, 0:16], srs(tc), 1.0, ALU.mult, ALU.mult),
                reads=[bStat], writes=[bStat])
        if KC_ON >= 4:
            P.add("vector", lambda v: v.tensor_copy(
                SQT.s(0, D).rearrange("p (h d) -> p h d", h=16),
                stat[:, 16:32].unsqueeze(2).broadcast_to((128, 16, 64))),
                reads=[bStat], writes=[bSQT])
        if KC_ON >= 5:
            P.add("vector", lambda v: v.tensor_tensor(scr[:, 0:D], scr[:, 0:D], SQT.s(0, D),
                                                      ALU.mult), reads=[bSQT, bScr],
                  writes=[bScr])
        P.add("vector", lambda v, tc=tc: v.tensor_tensor(
            KT.s(tc * D, (tc + 1) * D), scr[:, 0:D], KNB.s(0, D), ALU.mult),
            reads=[bScr, bKNB], writes=[bKT])
        if tc == 0 and "vearly" in mk.dbg:
            ape = mk.dout("dbg_vearly", [128, 1024], BF16)
            P.add("sync", lambda sy, ape=ape: sy.dma_start(ape, VT.s(0, 1024)),
                  reads=[bVT], dma=1)
        if tc == 0 and "vraw" in mk.dbg:
            apv = mk.dout("dbg_vraw", [128, 512], BF16)
            P.add("vector", lambda v: v.tensor_copy(scrb[:, 0:512], PB[0][:, :]),
                  reads=[bPB[0]], writes=[bScrb])
            P.add("sync", lambda sy, apv=apv: sy.dma_start(apv, scrb[:, 0:512]),
                  reads=[bScrb], dma=1)

        if not AT_ON:
            continue
        P.add("vector", lambda v, tc=tc: v.tensor_scalar(
            scr[:, 0:32], PB[4][:, 0:32], srs(tc), 1.0, ALU.mult, ALU.mult),
            reads=[bPB[4], bStat], writes=[bScr])
        P.add("vector", lambda v: v.tensor_tensor(scr[:, 0:32], scr[:, 0:32], ATB.s(0, 32),
                                                  ALU.add), reads=[bATB, bScr], writes=[bScr])
        P.add("scalar", lambda s: s.activation(ALB.s(0, 16), scr[:, 0:16], AF.Sigmoid),
              reads=[bScr], writes=[bALB])
        P.add("scalar", lambda s: s.activation(scr[:, 32:48], scr[:, 16:32], AF.Sigmoid),
              reads=[bScr], writes=[bScr])
        P.add("vector", lambda v, tc=tc: v.tensor_scalar_mul(
            THETA.s(tc * 16, (tc + 1) * 16), scr[:, 32:48], 0.02),
            reads=[bScr], writes=[bTHETA])
        P.add("tensor", lambda t: t.matmul(
            PB[5][0:1, 0:16], ones_bf[:, 0:1], ALB.s(0, 16),
            start=True, stop=True),
            reads=[bALB, bConst], writes=[bPB[5]])
        if tc == 0:
            P.add("vector", lambda v: v.tensor_copy(AMACC.s(0, 16)[0:1, :], PB[5][0:1, 0:16]),
                  reads=[bPB[5]], writes=[bAMACC])
        else:
            P.add("vector", lambda v: v.tensor_tensor(
                AMACC.s(0, 16)[0:1, :], AMACC.s(0, 16)[0:1, :], PB[5][0:1, 0:16], ALU.add),
                reads=[bPB[5], bAMACC], writes=[bAMACC])

    AMSR, bAMSR = A.alloc("l1amsr", 16, F32R)
    if not AT_ON:
        P.add("gpsimd", lambda g: g.memset(AMB.s(0, 16), 0.5), writes=[bAMB])
    if AT_ON:
        P.add("vector", lambda v: v.tensor_scalar(
            scr[0:1, 0:16], AMACC.s(0, 16)[0:1, :], -1.0 / T, 1.0, ALU.mult, ALU.add),
            reads=[bAMACC], writes=[bScr])
    if AT_ON:
        P.add("scalar", lambda s: s.activation(AMSR.s(0, 16)[0:1, :], scr[0:1, 0:16],
                                               AF.Identity),
              reads=[bScr], writes=[bAMSR])
        P.add("tensor", lambda t: t.matmul(PB[5][:, 0:16], OROW[0:1, :], AMSR.s(0, 16)[0:1, :],
                                           start=True, stop=True),
              reads=[bAMSR, bConst], writes=[bPB[5]])
        P.add("vector", lambda v: v.tensor_copy(AMB.s(0, 16), PB[5][:, 0:16]),
              reads=[bPB[5]], writes=[bAMB])

    A.drop(WKC, bWKC)
    A.drop(WVC, bWVC)
    A.drop(ATW, bATW)
    A.drop(KNB, bKNB)
    A.drop(ATB, bATB)
    A.drop(AMSR, bAMSR)
    A.drop(AMACC, bAMACC)
    A.drop(ALB, bALB)
    for (vv, bb) in XNFt:
        A.drop(vv, bb)

    if "kt" in mk.dbg:
        tap_tmv(P, mk, "kt", KT, bKT)
    if "vt" in mk.dbg:
        tap_tmv(P, mk, "vt", VT, bVT)
    if "theta" in mk.dbg:
        tap_tmv(P, mk, "theta", THETA, bTHETA, cols=16)
    if "amb" in mk.dbg:
        tap_cols(P, mk, "amb", lambda: AMB.s(0, 16), 128, 16, bAMB, F32)
    if upto < 5:
        A.drop(KT, bKT); A.drop(VT, bVT); A.drop(THETA, bTHETA)
        A.drop(SQT, bSQT); A.drop(SW, bSW); A.drop(AMB, bAMB)
        return

    W0sb, bW0 = A.alloc("l1w0", 256, BF16)
    W1C, bW1C = A.alloc("l1w1c", 128, BF16)
    W1T, bW1T = A.alloc("l1w1t", 256, BF16)
    NW0, bNW0 = A.alloc("l1nw0", NH * 256, BF16)
    NW1, bNW1 = A.alloc("l1nw1", NH * 128, BF16)
    P.add("gpsimd", lambda g: g.dma_start(W0sb.s(0, 256)[0:64, :], mw0_d), writes=[bW0], dma=1)
    P.add("gpsimd", lambda g: g.dma_start(
        W1C.s(0, 128).rearrange("p (c j) -> p c j", c=2),
        mw1_d.rearrange("(c p) j -> p c j", p=128)), writes=[bW1C], dma=1)
    for c in range(2):
        P.add("tensor", lambda t, c=c: t.transpose(
            PBT[0][0:64, 0:128], W1C.s(c * 64, (c + 1) * 64), ident[:, :]),
            reads=[bW1C, bConst], writes=[bPBT[0]])
        P.add("vector", lambda v, c=c: v.tensor_copy(
            W1T.s(c * 128, (c + 1) * 128)[0:64, :], PBT[0][0:64, 0:128]),
            reads=[bPBT[0]], writes=[bW1T])

    KFM = [A.alloc(f"l1kfm{i}", 128, BF16) for i in range(2)]
    A1F = [A.alloc(f"l1a1f{i}", 256, BF16) for i in range(2)]
    A1T_ = [A.alloc(f"l1a1t{i}", 256, BF16) for i in range(2)]
    GD = [A.alloc(f"l1gd{i}", 256, BF16) for i in range(2)]
    P1E = [A.alloc(f"l1p1e{i}", 64, F32) for i in range(2)]
    HN = [A.alloc(f"l1hn{i}", 64, F32) for i in range(2)]
    DP = [A.alloc(f"l1dp{i}", 64, F32) for i in range(2)]
    DHf = [A.alloc(f"l1dhf32{i}", 64, F32) for i in range(2)]
    DHB = [A.alloc(f"l1dhb{i}", 64, BF16) for i in range(2)]
    DHF = [A.alloc(f"l1dhfm{i}", 128, BF16) for i in range(2)]
    DH0 = [A.alloc(f"l1dh0{i}", 256, BF16) for i in range(2)]
    SC = [A.alloc(f"l1sc{i}", 128, F32) for i in range(2)]
    G0E, bG0E = A.alloc("l1g0e", 256, F32)
    G1B, bG1B = A.alloc("l1g1b", 256, BF16)
    SQ, bSQ = A.alloc("l1sq", 8, F32)
    SQR, bSQR = A.alloc("l1sqr", 4, F32R)
    CFR, bCFR = A.alloc("l1cfr", 4, F32R)
    NCF, bNCF = A.alloc("l1ncf", 1, F32)
    TV, bTV = A.alloc("l1tv", 256, F32)

    for h in range(NH):
        for tc in range(NTC):
            ki = tc % 2
            KFMv, bKFMv = KFM[ki]
            A1Fv, bA1Fv = A1F[ki]
            A1Tv, bA1Tv = A1T_[ki]
            GDv, bGDv = GD[ki]
            P1Ev, bP1Ev = P1E[ki]
            HNv, bHNv = HN[ki]
            DPv, bDPv = DP[ki]
            DHfv, bDHfv = DHf[ki]
            DHBv, bDHBv = DHB[ki]
            DHFv, bDHFv = DHF[ki]
            DH0v, bDH0v = DH0[ki]
            SCv, bSCv = SC[ki]
            ko = tc * D + h * HD
            tho = tc * 16 + h
            P.add("tensor", lambda t, ki=ki, ko=ko: t.transpose(
                PBT[ki][0:64, 0:128], KT.s(ko, ko + HD), ident[:, :]),
                reads=[bKT, bConst], writes=[bPBT[ki]])
            P.add("vector", lambda v, ki=ki, KFMv=KFMv: v.tensor_copy(
                KFMv.s(0, 128)[0:64, :], PBT[ki][0:64, 0:128]),
                reads=[bPBT[ki]], writes=[bKFMv])

            def ffwd(t, KFMv=KFMv):
                kf = KFMv.s(0, 128)[0:64, :]
                t.matmul(PB[0][:, 0:128], W0sb.s(0, 128)[0:64, :], kf, start=True, stop=True)
                t.matmul(PB[0][:, 128:256], W0sb.s(128, 256)[0:64, :], kf, start=True, stop=True)
                return t.matmul(PB[2][:, 0:256], kf, W0sb.s(0, 256)[0:64, :],
                                start=True, stop=True)
            P.add("tensor", ffwd, reads=[bW0, bKFMv], writes=[bPB[0], bPB[2]])
            P.add("scalar", lambda s, A1Fv=A1Fv: s.activation(A1Fv.s(0, 256), PB[0][:, 0:256],
                                                              AF.Gelu),
                  reads=[bPB[0]], writes=[bA1Fv])
            P.add("scalar", lambda s, A1Tv=A1Tv: s.activation(A1Tv.s(0, 256), PB[2][:, 0:256],
                                                              AF.Gelu),
                  reads=[bPB[2]], writes=[bA1Tv])
            P.add("scalar", lambda s, GDv=GDv: s.activation(GDv.s(0, 256), PB[2][:, 0:256],
                                                            AF.Derivative_Gelu),
                  reads=[bPB[2]], writes=[bGDv])

            def fpre1(t, A1Fv=A1Fv):
                t.matmul(PB[1][:, 0:64], A1Fv.s(0, 128), W1C.s(0, 64), start=True, stop=False)
                return t.matmul(PB[1][:, 0:64], A1Fv.s(128, 256), W1C.s(64, 128),
                                start=False, stop=True)
            P.add("tensor", fpre1, reads=[bA1Fv, bW1C], writes=[bPB[1]])
            P.add("scalar", lambda s, P1Ev=P1Ev: s.activation(P1Ev.s(0, 64), PB[1][:, 0:64],
                                                              AF.Identity),
                  reads=[bPB[1]], writes=[bP1Ev])
            P.add("vector", lambda v, P1Ev=P1Ev, SCv=SCv: v.scalar_tensor_tensor(
                SCv.s(0, 64), P1Ev.s(0, 64), 1.0, P1Ev.s(0, 64), ALU.mult, ALU.mult,
                accum_out=SCv.s(64, 65)), reads=[bP1Ev], writes=[bSCv])
            P.add("scalar", lambda s, SCv=SCv: s.activation(
                SCv.s(65, 66), SCv.s(64, 65), AF.Sqrt, bias=cvals[:, 2:3], scale=1.0 / HD),
                reads=[bSCv, bConst], writes=[bSCv])
            P.add("vector", lambda v, SCv=SCv: v.reciprocal(SCv.s(66, 67), SCv.s(65, 66)),
                  reads=[bSCv], writes=[bSCv])
            P.add("vector", lambda v, P1Ev=P1Ev, HNv=HNv, SCv=SCv: v.tensor_scalar(
                HNv.s(0, 64), P1Ev.s(0, 64), SCv.s(66, 67), 1.0, ALU.mult, ALU.mult),
                reads=[bP1Ev, bSCv], writes=[bHNv])
            P.add("vector", lambda v, HNv=HNv, DPv=DPv, ko=ko: v.tensor_tensor(
                DPv.s(0, 64), HNv.s(0, 64), KT.s(ko, ko + HD), ALU.add),
                reads=[bHNv, bKT], writes=[bDPv])
            P.add("vector", lambda v, DPv=DPv, ko=ko: v.tensor_tensor(
                DPv.s(0, 64), DPv.s(0, 64), VT.s(ko, ko + HD), ALU.subtract),
                reads=[bDPv, bVT], writes=[bDPv])
            P.add("vector", lambda v, DPv=DPv, tho=tho: v.tensor_scalar(
                DPv.s(0, 64), DPv.s(0, 64), THETA.s(tho, tho + 1), 1.0, ALU.mult, ALU.mult),
                reads=[bDPv, bTHETA], writes=[bDPv])
            P.add("vector", lambda v, DPv=DPv, HNv=HNv, SCv=SCv: v.scalar_tensor_tensor(
                SCv.s(0, 64), DPv.s(0, 64), 1.0, HNv.s(0, 64), ALU.mult, ALU.mult,
                accum_out=SCv.s(67, 68)), reads=[bDPv, bHNv], writes=[bSCv])
            P.add("vector", lambda v, HNv=HNv, SCv=SCv: v.tensor_scalar(
                SCv.s(0, 64), HNv.s(0, 64), SCv.s(67, 68), 1.0 / HD, ALU.mult, ALU.mult),
                reads=[bHNv, bSCv], writes=[bSCv])
            P.add("vector", lambda v, DPv=DPv, SCv=SCv, DHfv=DHfv: v.tensor_tensor(
                DHfv.s(0, 64), DPv.s(0, 64), SCv.s(0, 64), ALU.subtract),
                reads=[bDPv, bSCv], writes=[bDHfv])
            P.add("vector", lambda v, DHfv=DHfv, SCv=SCv: v.tensor_scalar(
                DHfv.s(0, 64), DHfv.s(0, 64), SCv.s(66, 67), 1.0, ALU.mult, ALU.mult),
                reads=[bDHfv, bSCv], writes=[bDHfv])
            P.add("scalar", lambda s, DHfv=DHfv, DHBv=DHBv: s.activation(
                DHBv.s(0, 64), DHfv.s(0, 64), AF.Identity),
                reads=[bDHfv], writes=[bDHBv])
            P.add("tensor", lambda t, ki=ki, DHBv=DHBv: t.transpose(
                PBT[ki][0:64, 128:256], DHBv.s(0, 64), ident[:, :]),
                reads=[bDHBv, bConst], writes=[bPBT[ki]])
            P.add("vector", lambda v, ki=ki, DHFv=DHFv: v.tensor_copy(
                DHFv.s(0, 128)[0:64, :], PBT[ki][0:64, 128:256]),
                reads=[bPBT[ki]], writes=[bDHFv])
            P.add("tensor", lambda t, DHBv=DHBv, A1Tv=A1Tv, tc=tc: t.matmul(
                PB[4][0:64, 0:256], DHBv.s(0, 64), A1Tv.s(0, 256),
                start=(tc == 0), stop=(tc == NTC - 1)),
                reads=[bDHBv, bA1Tv], writes=[bPB[4]])
            P.add("tensor", lambda t, DHFv=DHFv: t.matmul(
                PB[3][:, 0:256], DHFv.s(0, 128)[0:64, :], W1T.s(0, 256)[0:64, :],
                start=True, stop=True),
                reads=[bDHFv, bW1T], writes=[bPB[3]])
            P.add("vector", lambda v, DH0v=DH0v, GDv=GDv: v.tensor_tensor(
                DH0v.s(0, 256), PB[3][:, 0:256], GDv.s(0, 256), ALU.mult),
                reads=[bPB[3], bGDv], writes=[bDH0v])
            P.add("tensor", lambda t, DH0v=DH0v, ko=ko, tc=tc: t.matmul(
                PB[5][0:64, 0:256], KT.s(ko, ko + HD), DH0v.s(0, 256),
                start=(tc == 0), stop=(tc == NTC - 1)),
                reads=[bKT, bDH0v], writes=[bPB[5]])

        P.add("scalar", lambda s: s.activation(G0E.s(0, 256)[0:64, :], PB[5][0:64, 0:256],
                                               AF.Identity), reads=[bPB[5]], writes=[bG0E])
        P.add("scalar", lambda s: s.activation(G1B.s(0, 256)[0:64, :], PB[4][0:64, 0:256],
                                               AF.Identity), reads=[bPB[4]], writes=[bG1B])
        P.add("vector", lambda v: v.scalar_tensor_tensor(
            TV.s(0, 256)[0:64, :], G0E.s(0, 256)[0:64, :], 1.0, G0E.s(0, 256)[0:64, :],
            ALU.mult, ALU.mult, accum_out=SQ.s(0, 1)[0:64, :]),
            reads=[bG0E], writes=[bSQ, bTV])
        P.add("vector", lambda v: v.scalar_tensor_tensor(
            TV.s(0, 256)[0:64, :], PB[4][0:64, 0:256], 1.0, G1B.s(0, 256)[0:64, :],
            ALU.mult, ALU.mult, accum_out=SQ.s(1, 2)[0:64, :]),
            reads=[bPB[4], bG1B], writes=[bSQ, bTV])
        P.add("vector", lambda v: v.tensor_tensor(
            SQ.s(2, 3)[0:64, :], SQ.s(0, 1)[0:64, :], SQ.s(1, 2)[0:64, :], ALU.add),
            reads=[bSQ], writes=[bSQ])
        P.add("scalar", lambda s: s.activation(
            SQR.s(0, 4)[0:64, :], SQ.s(2, 3)[0:64, :].broadcast_to((64, 4)), AF.Identity),
            reads=[bSQ], writes=[bSQR])
        P.add("tensor", lambda t: t.matmul(PB[0][0:4, 0:4], SQR.s(0, 4)[0:64, :],
                                           ONER[0:64, 0:4], start=True, stop=True),
              reads=[bSQR, bConst], writes=[bPB[0]])
        P.add("scalar", lambda s: s.activation(SQ.s(3, 4)[0:1, :], PB[0][0:1, 0:1],
                                               AF.Sqrt, bias=cvals[0:1, 3:4]),
              reads=[bPB[0], bConst], writes=[bSQ])
        P.add("vector", lambda v: v.tensor_single_scalar(
            SQ.s(3, 4)[0:1, :], SQ.s(3, 4)[0:1, :], 1e-6, ALU.add),
            reads=[bSQ], writes=[bSQ])
        P.add("vector", lambda v: v.reciprocal(SQ.s(4, 5)[0:1, :], SQ.s(3, 4)[0:1, :]),
              reads=[bSQ], writes=[bSQ])
        P.add("vector", lambda v: v.tensor_scalar_mul(SQ.s(4, 5)[0:1, :], SQ.s(4, 5)[0:1, :],
                                                      10.0), reads=[bSQ], writes=[bSQ])
        P.add("vector", lambda v: v.tensor_single_scalar(
            SQ.s(4, 5)[0:1, :], SQ.s(4, 5)[0:1, :], 1.0, ALU.min),
            reads=[bSQ], writes=[bSQ])
        P.add("scalar", lambda s: s.activation(
            CFR.s(0, 4)[0:1, :], SQ.s(4, 5)[0:1, :].broadcast_to((1, 4)), AF.Identity),
            reads=[bSQ], writes=[bCFR])
        P.add("tensor", lambda t: t.matmul(PB[0][:, 0:4], OROW[0:1, :], CFR.s(0, 4)[0:1, :],
                                           start=True, stop=True),
              reads=[bCFR, bConst], writes=[bPB[0]])
        P.add("vector", lambda v: v.tensor_scalar_mul(NCF.s(0, 1), PB[0][:, 0:1], -1.0),
              reads=[bPB[0]], writes=[bNCF])
        P.add("vector", lambda v: v.tensor_scalar(
            G0E.s(0, 256)[0:64, :], G0E.s(0, 256)[0:64, :], NCF.s(0, 1)[0:64, :], 1.0,
            ALU.mult, ALU.mult), reads=[bG0E, bNCF], writes=[bG0E])
        P.add("vector", lambda v, h=h: v.scalar_tensor_tensor(
            NW0.s(h * 256, (h + 1) * 256)[0:64, :], W0sb.s(0, 256)[0:64, :],
            AMB.s(h, h + 1)[0:64, :], G0E.s(0, 256)[0:64, :], ALU.mult, ALU.add),
            reads=[bW0, bAMB, bG0E], writes=[bNW0])
        for c in range(2):
            P.add("tensor", lambda t, c=c: t.transpose(
                PBT[0][:, 256 + c * 64:256 + (c + 1) * 64],
                G1B.s(c * 128, (c + 1) * 128)[0:64, :], ident[0:64, 0:64]),
                reads=[bG1B, bConst], writes=[bPBT[0]])
            P.add("vector", lambda v, c=c: v.tensor_scalar(
                TV.s(0, 64), PBT[0][:, 256 + c * 64:256 + (c + 1) * 64],
                NCF.s(0, 1), 1.0, ALU.mult, ALU.mult),
                reads=[bPBT[0], bNCF], writes=[bTV])
            P.add("vector", lambda v, c=c, h=h: v.scalar_tensor_tensor(
                NW1.s(h * 128 + c * 64, h * 128 + (c + 1) * 64), W1C.s(c * 64, (c + 1) * 64),
                AMB.s(h, h + 1), TV.s(0, 64), ALU.mult, ALU.add),
                reads=[bW1C, bAMB, bTV], writes=[bNW1])

    if "nw0" in mk.dbg:
        tap_cols(P, mk, "nw0", lambda: NW0.s(0, NH * 256), 128, NH * 256, bNW0, BF16)
    if "nw1" in mk.dbg:
        tap_cols(P, mk, "nw1", lambda: NW1.s(0, NH * 128), 128, NH * 128, bNW1, BF16)

    for lst in (KFM, A1F, A1T_, GD, P1E, HN, DP, DHf, DHB, DHF, DH0, SC):
        for (vv, bb) in lst:
            A.drop(vv, bb)
    A.drop(G0E, bG0E); A.drop(G1B, bG1B); A.drop(SQ, bSQ); A.drop(SQR, bSQR)
    A.drop(CFR, bCFR); A.drop(NCF, bNCF); A.drop(TV, bTV)
    A.drop(KT, bKT); A.drop(VT, bVT); A.drop(THETA, bTHETA)
    A.drop(W0sb, bW0); A.drop(W1C, bW1C); A.drop(W1T, bW1T)
    A.drop(SW, bSW); A.drop(AMB, bAMB)
    if upto < 6:
        A.drop(SQT, bSQT)
        A.drop(NW0, bNW0); A.drop(NW1, bNW1)
        return

    QT, bQT = A.alloc("l1qt", NTC * D, BF16)
    QNB, bQNB = A.alloc("l1qnb", D, BF16)
    RW, bRW = A.alloc("l1rw", 8, F32)
    P.add("sync", lambda sy: sy.dma_start(RW.s(0, 8), rnw_d.rearrange("(c p) -> p c", p=128),
                                          allow_slow_non_contiguous=True),
          writes=[bRW], dma=1)
    P.add("gpsimd", lambda g: g.dma_start(scrb[0:1, 0:64], qnw_d), writes=[bScrb], dma=1)
    P.add("tensor", lambda t: t.matmul(PB[0][:, 0:64], ones_bf[0:1, 0:128], scrb[0:1, 0:64],
                                       start=True, stop=True),
          reads=[bScrb, bConst], writes=[bPB[0]])
    P.add("vector", lambda v: v.tensor_copy(
        QNB.s(0, D).rearrange("p (h d) -> p h d", h=16),
        PB[0][:, 0:64].unsqueeze(1).broadcast_to((128, 16, 64))),
        reads=[bPB[0]], writes=[bQNB])
    WQC, bWQC = A.alloc("l1wqc", 8 * D, BF16)
    for dc in range(8):
        P.add("gpsimd", lambda g, dc=dc: g.dma_start(
            WQC.s(dc * D, (dc + 1) * D), wq_d[dc * 128:(dc + 1) * 128, :]),
            writes=[bWQC], dma=1)
        P.add("vector", lambda v, dc=dc: v.tensor_scalar(
            WQC.s(dc * D, (dc + 1) * D), WQC.s(dc * D, (dc + 1) * D),
            RW.s(dc, dc + 1), 1.0, ALU.mult, ALU.mult),
            reads=[bWQC, bRW], writes=[bWQC])
    XNFq = [A.alloc(f"l1xnfq{i}", 8 * 128, BF16) for i in range(2)]
    for tc in range(NTC):
        XNv, bXNv = XNFq[tc % 2]
        P.add("sync", lambda sy, tc=tc: sy.dma_start(
            scrx[:, 0:D], xnsp.rearrange("p (c n) -> p c n", c=NTC)[:, tc, :]),
            reads=[bXnspD, bScrx], writes=[bScrx], dma=1)
        for dc in range(8):
            P.add("tensor", lambda t, dc=dc: t.transpose(
                PBT[dc % 2][:, 0:128], scrx[:, dc * 128:(dc + 1) * 128], ident[:, :]),
                reads=[bScrx, bConst], writes=[bPBT[dc % 2]])
            P.add("vector", lambda v, dc=dc, XNv=XNv: v.tensor_copy(
                XNv.s(dc * 128, (dc + 1) * 128), PBT[dc % 2][:, 0:128]),
                reads=[bPBT[dc % 2]], writes=[bXNv])

        def fq(t, XNv=XNv):
            ins = None
            for dc in range(8):
                lh = XNv.s(dc * 128, (dc + 1) * 128)
                st, sp = (dc == 0), (dc == 7)
                t.matmul(PB[0][:, :], lh, WQC.s(dc * D, dc * D + 512), start=st, stop=sp)
                ins = t.matmul(PB[1][:, :], lh, WQC.s(dc * D + 512, (dc + 1) * D),
                               start=st, stop=sp)
            return ins
        P.add("tensor", fq, reads=[bXNv, bWQC], writes=[bPB[0], bPB[1]])
        P.add("scalar", lambda s: s.activation(scr[:, 0:512], PB[0][:, :], AF.Identity),
              reads=[bPB[0]], writes=[bScr])
        P.add("scalar", lambda s: s.activation(scr[:, 512:1024], PB[1][:, :], AF.Identity),
              reads=[bPB[1]], writes=[bScr])
        P.add("vector", lambda v: v.scalar_tensor_tensor(
            SQT.s(0, D), scr[:, 0:D], 1.0, scr[:, 0:D], ALU.mult, ALU.mult),
            reads=[bScr], writes=[bSQT])
        P.add("vector", lambda v: v.reduce_sum(
            stat[:, 0:16].unsqueeze(2), SQT.s(0, D).rearrange("p (h d) -> p h d", h=16),
            axis=AXX), reads=[bSQT], writes=[bStat])
        P.add("vector", lambda v, tc=tc: v.tensor_tensor(
            stat[:, 16:17], srs(tc), srs(tc), ALU.mult), reads=[bStat], writes=[bStat])
        P.add("vector", lambda v: v.tensor_scalar(
            stat[:, 0:16], stat[:, 0:16], stat[:, 16:17], 1.0 / HD, ALU.mult, ALU.mult),
            reads=[bStat], writes=[bStat])
        P.add("scalar", lambda s: s.activation(stat[:, 16:32], stat[:, 0:16], AF.Sqrt,
                                               bias=cvals[:, 1:2]),
              reads=[bStat, bConst], writes=[bStat])
        P.add("vector", lambda v: v.reciprocal(stat[:, 0:16], stat[:, 16:32]),
              reads=[bStat], writes=[bStat])
        P.add("vector", lambda v, tc=tc: v.tensor_scalar(
            stat[:, 16:32], stat[:, 0:16], srs(tc), 1.0, ALU.mult, ALU.mult),
            reads=[bStat], writes=[bStat])
        P.add("vector", lambda v: v.tensor_copy(
            SQT.s(0, D).rearrange("p (h d) -> p h d", h=16),
            stat[:, 16:32].unsqueeze(2).broadcast_to((128, 16, 64))),
            reads=[bStat], writes=[bSQT])
        P.add("vector", lambda v: v.tensor_tensor(scr[:, 0:D], scr[:, 0:D], SQT.s(0, D),
                                                  ALU.mult), reads=[bSQT, bScr], writes=[bScr])
        P.add("vector", lambda v, tc=tc: v.tensor_tensor(
            QT.s(tc * D, (tc + 1) * D), scr[:, 0:D], QNB.s(0, D), ALU.mult),
            reads=[bScr, bQNB], writes=[bQT])
    A.drop(WQC, bWQC)
    A.drop(QNB, bQNB)
    A.drop(RW, bRW)
    A.drop(SQT, bSQT)
    for (vv, bb) in XNFq:
        A.drop(vv, bb)
    if "qt" in mk.dbg:
        tap_tmv(P, mk, "qt", QT, bQT)

    QFM = [A.alloc(f"l1qfm{i}", 128, BF16) for i in range(2)]
    A1R = [A.alloc(f"l1a1r{i}", 256, BF16) for i in range(2)]
    P1R = [A.alloc(f"l1p1r{i}", 64, F32) for i in range(2)]
    SCr = [A.alloc(f"l1scr{i}", 128, F32) for i in range(2)]
    for h in range(NH):
        for tc in range(NTC):
            ki = tc % 2
            QFMv, bQFMv = QFM[ki]
            A1Rv, bA1Rv = A1R[ki]
            P1Rv, bP1Rv = P1R[ki]
            SCv, bSCv = SCr[ki]
            qo = tc * D + h * HD
            P.add("tensor", lambda t, ki=ki, qo=qo: t.transpose(
                PBT[ki][0:64, 0:128], QT.s(qo, qo + HD), ident[:, :]),
                reads=[bQT, bConst], writes=[bPBT[ki]])
            P.add("vector", lambda v, ki=ki, QFMv=QFMv: v.tensor_copy(
                QFMv.s(0, 128)[0:64, :], PBT[ki][0:64, 0:128]),
                reads=[bPBT[ki]], writes=[bQFMv])

            def frf(t, QFMv=QFMv, h=h):
                qf = QFMv.s(0, 128)[0:64, :]
                t.matmul(PB[0][:, 0:128], NW0.s(h * 256, h * 256 + 128)[0:64, :], qf,
                         start=True, stop=True)
                return t.matmul(PB[0][:, 128:256], NW0.s(h * 256 + 128, (h + 1) * 256)[0:64, :],
                                qf, start=True, stop=True)
            P.add("tensor", frf, reads=[bNW0, bQFMv], writes=[bPB[0]])
            P.add("scalar", lambda s, A1Rv=A1Rv: s.activation(A1Rv.s(0, 256), PB[0][:, 0:256],
                                                              AF.Gelu),
                  reads=[bPB[0]], writes=[bA1Rv])

            def fr1(t, A1Rv=A1Rv, h=h):
                t.matmul(PB[1][:, 0:64], A1Rv.s(0, 128), NW1.s(h * 128, h * 128 + 64),
                         start=True, stop=False)
                return t.matmul(PB[1][:, 0:64], A1Rv.s(128, 256),
                                NW1.s(h * 128 + 64, (h + 1) * 128), start=False, stop=True)
            P.add("tensor", fr1, reads=[bA1Rv, bNW1], writes=[bPB[1]])
            P.add("scalar", lambda s, P1Rv=P1Rv: s.activation(P1Rv.s(0, 64), PB[1][:, 0:64],
                                                              AF.Identity),
                  reads=[bPB[1]], writes=[bP1Rv])
            P.add("vector", lambda v, P1Rv=P1Rv, SCv=SCv: v.scalar_tensor_tensor(
                SCv.s(0, 64), P1Rv.s(0, 64), 1.0, P1Rv.s(0, 64), ALU.mult, ALU.mult,
                accum_out=SCv.s(64, 65)), reads=[bP1Rv], writes=[bSCv])
            P.add("scalar", lambda s, SCv=SCv: s.activation(
                SCv.s(65, 66), SCv.s(64, 65), AF.Sqrt, bias=cvals[:, 2:3], scale=1.0 / HD),
                reads=[bSCv, bConst], writes=[bSCv])
            P.add("vector", lambda v, SCv=SCv: v.reciprocal(SCv.s(66, 67), SCv.s(65, 66)),
                  reads=[bSCv], writes=[bSCv])
            P.add("vector", lambda v, P1Rv=P1Rv, SCv=SCv: v.tensor_scalar(
                SCv.s(0, 64), P1Rv.s(0, 64), SCv.s(66, 67), 1.0, ALU.mult, ALU.mult),
                reads=[bP1Rv, bSCv], writes=[bSCv])
            P.add("vector", lambda v, SCv=SCv, qo=qo: v.tensor_tensor(
                SCv.s(0, 64), SCv.s(0, 64), QT.s(qo, qo + HD), ALU.add),
                reads=[bSCv, bQT], writes=[bSCv])
            P.add("vector", lambda v, SCv=SCv, qo=qo: v.tensor_tensor(
                XR[:, qo:qo + HD], XR[:, qo:qo + HD], SCv.s(0, 64), ALU.add),
                reads=[bSCv], writes=[bXR])

    for lst in (QFM, A1R, P1R, SCr):
        for (vv, bb) in lst:
            A.drop(vv, bb)
    A.drop(QT, bQT)
    A.drop(NW0, bNW0)
    A.drop(NW1, bNW1)
    if "xr4" in mk.dbg:
        apo = mk.dout("dbg_xr4", [T, D], F32)
        P.add("sync", lambda sy, apo=apo: sy.dma_start(
            apo.rearrange("(c p) n -> p c n", p=128),
            XR[:, :].rearrange("p (c n) -> p c n", c=NTC)), reads=[bXR], dma=1)


def l1_ffn(P, mk, A, XR, bXR, scr, bScr, scrx, bScrx, ident, ones_bf,
           cvals, bConst, PB, bPB, PBT, bPBT, xnsp, bXnspD,
           wgate_d, wup_d, wdown_d):
    S127 = 1.0 / 127.0
    XNF, bXNF = A.alloc("l8xnf", NTC * D, BF16)
    for tc in range(NTC):
        P.add("sync", lambda sy, tc=tc: sy.dma_start(
            scrx[:, 0:D], xnsp.rearrange("p (c n) -> p c n", c=NTC)[:, tc, :]),
            reads=[bXnspD, bScrx], writes=[bScrx], dma=1)
        for dc in range(8):
            P.add("tensor", lambda t, dc=dc: t.transpose(
                PBT[dc % 2][:, 0:128], scrx[:, dc * 128:(dc + 1) * 128], ident[:, :]),
                reads=[bScrx, bConst], writes=[bPBT[dc % 2]])
            P.add("vector", lambda v, tc=tc, dc=dc: v.tensor_copy(
                XNF.s(tc * D + dc * 128, tc * D + (dc + 1) * 128), PBT[dc % 2][:, 0:128]),
                reads=[bPBT[dc % 2]], writes=[bXNF])

    xnf_r = XNF.s(0, NTC * D).rearrange("p (tc dc t) -> p tc dc t", tc=NTC, dc=8)

    _offs = {}

    def _chk(name, v):
        if name in _offs:
            assert _offs[name] == v.off, (name, _offs[name], v.off)
        _offs[name] = v.off
        return v

    for hb in range(6):
        GN, bGN = A.alloc("l8gn", 4 * D, BF16); _chk("gn", GN)
        UN, bUN = A.alloc("l8un", 4 * D, BF16); _chk("un", UN)
        DN, bDN = A.alloc("l8dn", 8 * 512, BF16); _chk("dn", DN)
        for k in range(4):
            P.add("gpsimd", lambda g, hb=hb, k=k, GN=GN: g.dma_start(
                GN.s(k * D, (k + 1) * D),
                wgate_d[hb * 512 + k * 128:hb * 512 + (k + 1) * 128, :]),
                writes=[bGN], dma=1)
            P.add("gpsimd", lambda g, hb=hb, k=k, UN=UN: g.dma_start(
                UN.s(k * D, (k + 1) * D),
                wup_d[hb * 512 + k * 128:hb * 512 + (k + 1) * 128, :]),
                writes=[bUN], dma=1)
        for dcj in range(8):
            P.add("gpsimd", lambda g, hb=hb, dcj=dcj, DN=DN: g.dma_start(
                DN.s(dcj * 512, (dcj + 1) * 512),
                wdown_d[dcj * 128:(dcj + 1) * 128, hb * 512:(hb + 1) * 512]),
                writes=[bDN], dma=1)
        WGT, bWGT = A.alloc("l8wgt", 32 * 128, BF16); _chk("wgt", WGT)
        WUT, bWUT = A.alloc("l8wut", 32 * 128, BF16); _chk("wut", WUT)
        WDT, bWDT = A.alloc("l8wdt", 4 * D, BF16); _chk("wdt", WDT)
        ti = 0
        for k in range(4):
            for dc in range(8):
                P.add("tensor", lambda t, k=k, dc=dc, ti=ti, GN=GN: t.transpose(
                    PBT[ti % 2][:, 0:128], GN.s(k * D + dc * 128, k * D + (dc + 1) * 128),
                    ident[:, :]), reads=[bGN, bConst], writes=[bPBT[ti % 2]])
                P.add("vector", lambda v, k=k, dc=dc, ti=ti, WGT=WGT: v.tensor_copy(
                    WGT.s((dc * 4 + k) * 128, (dc * 4 + k + 1) * 128), PBT[ti % 2][:, 0:128]),
                    reads=[bPBT[ti % 2]], writes=[bWGT])
                ti += 1
                P.add("tensor", lambda t, k=k, dc=dc, ti=ti, UN=UN: t.transpose(
                    PBT[ti % 2][:, 0:128], UN.s(k * D + dc * 128, k * D + (dc + 1) * 128),
                    ident[:, :]), reads=[bUN, bConst], writes=[bPBT[ti % 2]])
                P.add("vector", lambda v, k=k, dc=dc, ti=ti, WUT=WUT: v.tensor_copy(
                    WUT.s((dc * 4 + k) * 128, (dc * 4 + k + 1) * 128), PBT[ti % 2][:, 0:128]),
                    reads=[bPBT[ti % 2]], writes=[bWUT])
                ti += 1
        for dcj in range(8):
            for hs in range(4):
                P.add("tensor", lambda t, dcj=dcj, hs=hs, ti=ti, DN=DN: t.transpose(
                    PBT[ti % 2][:, 0:128], DN.s(dcj * 512 + hs * 128, dcj * 512 + (hs + 1) * 128),
                    ident[:, :]), reads=[bDN, bConst], writes=[bPBT[ti % 2]])
                P.add("vector", lambda v, dcj=dcj, hs=hs, ti=ti, WDT=WDT: v.tensor_copy(
                    WDT.s(hs * D + dcj * 128, hs * D + (dcj + 1) * 128), PBT[ti % 2][:, 0:128]),
                    reads=[bPBT[ti % 2]], writes=[bWDT])
                ti += 1
        A.drop(GN, bGN)
        A.drop(UN, bUN)
        A.drop(DN, bDN)
        HF, bHF = A.alloc("l8hf", 4 * 512, BF16); _chk("hf", HF)
        SIL, bSIL = A.alloc("l8sil", 512, BF16); _chk("sil", SIL)
        for tcg in range(4):
            for hsub in range(4):
                def fgu(t, tcg=tcg, hsub=hsub, WGT=WGT, WUT=WUT):
                    ins = None
                    for dcc in range(8):
                        st, sp = (dcc == 0), (dcc == 7)
                        rhs = xnf_r[:, tcg * 4:(tcg + 1) * 4, dcc, :]
                        t.matmul(PB[0][:, :], WGT.s((dcc * 4 + hsub) * 128,
                                                    (dcc * 4 + hsub + 1) * 128),
                                 rhs, start=st, stop=sp)
                        ins = t.matmul(PB[1][:, :], WUT.s((dcc * 4 + hsub) * 128,
                                                          (dcc * 4 + hsub + 1) * 128),
                                       rhs, start=st, stop=sp)
                    return ins
                P.add("tensor", fgu, reads=[bXNF, bWGT, bWUT], writes=[bPB[0], bPB[1]])
                P.add("scalar", lambda s, SIL=SIL: s.activation(
                    SIL.s(0, 512), PB[0][:, :], AF.Silu, scale=S127),
                    reads=[bPB[0]], writes=[bSIL])
                P.add("vector", lambda v, hsub=hsub, HF=HF, SIL=SIL: v.scalar_tensor_tensor(
                    HF.s(hsub * 512, (hsub + 1) * 512), PB[1][:, :], S127, SIL.s(0, 512),
                    ALU.mult, ALU.mult),
                    reads=[bPB[1], bSIL], writes=[bHF])
            for dh in range(2):
                for tsl in range(4):
                    pk = 2 + (tsl % 2)

                    def fdn(t, dh=dh, tsl=tsl, pk=pk, HF=HF, WDT=WDT):
                        ins = None
                        for hsub in range(4):
                            ins = t.matmul(PB[pk][:, :],
                                           HF.s(hsub * 512 + tsl * 128,
                                                hsub * 512 + (tsl + 1) * 128),
                                           WDT.s(hsub * D + dh * 512, hsub * D + (dh + 1) * 512),
                                           start=(hsub == 0), stop=(hsub == 3))
                        return ins
                    P.add("tensor", fdn, reads=[bHF, bWDT], writes=[bPB[pk]])
                    P.add("vector", lambda v, tcg=tcg, tsl=tsl, dh=dh, pk=pk: v.scalar_tensor_tensor(
                        XR[:, (tcg * 4 + tsl) * D + dh * 512:(tcg * 4 + tsl) * D + (dh + 1) * 512],
                        PB[pk][:, :], S127,
                        XR[:, (tcg * 4 + tsl) * D + dh * 512:(tcg * 4 + tsl) * D + (dh + 1) * 512],
                        ALU.mult, ALU.add),
                        reads=[bPB[pk]], writes=[bXR])
        A.drop(WGT, bWGT)
        A.drop(WUT, bWUT)
        A.drop(WDT, bWDT)
        A.drop(HF, bHF)
        A.drop(SIL, bSIL)
    A.drop(XNF, bXNF)
    if "xr5" in mk.dbg:
        apo = mk.dout("dbg_xr5", [T, D], F32)
        P.add("sync", lambda sy, apo=apo: sy.dma_start(
            apo.rearrange("(c p) n -> p c n", p=128),
            XR[:, :].rearrange("p (c n) -> p c n", c=NTC)), reads=[bXR], dma=1)
